# version 02
# baseline (speedup 1.0000x reference)
import numpy as np
import concourse.bass as bass
import concourse.mybir as mybir
from concourse.bass_utils import run_bass_kernel_spmd
from contextlib import ExitStack

F32 = mybir.dt.float32
BF16 = mybir.dt.bfloat16
AF = mybir.ActivationFunctionType
ALU = mybir.AluOpType
AX = mybir.AxisListType

D = 1024
HD = 64
NH = 16
HPC = 4
GROUPS4 = [[0, 1, 2, 3], [4, 5, 6, 7]]
GROUPS8 = [list(range(8))]
EXP_BIAS = -20.0
LN_EPS = 1e-5
RMS_EPS = 1e-6
RND_MAGIC = 12582912.0

ESEM = {"tensor": "pe", "vector": "dve", "scalar": "act", "gpsimd": "pool",
        "sync": "d_sync"}
DMASEM = {"sync": "d_sync", "scalar": "d_scal", "gpsimd": "d_pool"}


class Cfg:
    def __init__(self, S=8192, layers=(0, 1, 2, 3)):
        self.S = S
        self.T = S // 4
        self.TT = self.T // 128
        self.NQ = S // 512
        self.layers = tuple(layers)


class Ctx:

    def __init__(self, nc, es, cfg):
        self.nc = nc
        self.es = es
        self.cfg = cfg
        self.sem = {}
        self.cnt = {}
        self.waited = {e: {} for e in ESEM}
        for name in ("pe", "dve", "act", "pool", "d_sync", "d_scal", "d_pool", "cc"):
            self.sem[name] = es.enter_context(nc.semaphore(name))
            self.cnt[name] = 0
        self.q = {e: [] for e in ESEM}
        self.t = {}
        self.dram = {}

    SEM_ALIAS = {
        "sm_xq32a": "sm_bf20", "sm_xq32b": "sm_bf21",
        "sm_hld": "sm_bf1", "sm_w4src": "sm_dx0",
        "sm_gnw": "sm_hsel", "sm_gnb": "sm_kq",
        "sm_hsl": "sm_idnf", "sm_hslT": "sm_wrow",
        "sm_yld0": "sm_seed", "sm_yld1": "sm_xnstrm0",
        "sm_l1wg": "sm_wg2", "sm_l1wu": "sm_wgat",
        "sm_xqst": "sm_agwst", "sm_hqst": "sm_arst",
        "sm_hst": "sm_armst", "sm_q4st": "sm_armld",
        "sm_bf3": "sm_arld",
        "sm_agu": "sm_lra", "sm_agd": "sm_lrb",
        "sm_h2": "sm_qd",
        "sm_uw0": "sm_bk", "sm_uw1": "sm_bv",
        "sm_dw0": "sm_wq", "sm_dw1": "sm_wk",
        "sm_upb": "sm_borow", "sm_bx": "sm_fbias",
        "sm_bf22": "sm_bf20", "sm_bf23": "sm_bf21",
        "sm_rope0": "sm_lra", "sm_rope1": "sm_lrb",
        "cc_agu": "cc_ar", "cc_agd": "cc_arm",
    }

    def getsem(self, name):
        if name not in self.sem:
            self.sem[name] = self.es.enter_context(self.nc.semaphore(name))
            self.cnt[name] = 0
        return self.sem[name]

    def op(self, eng, build, deps=(), sem=None, step=None, dma=False, cc=False,
           dsem=None):
        if cc:
            assert dsem, "collectives need dsem"
            dsem = self.SEM_ALIAS.get(dsem, dsem)
            self.getsem(dsem)
            sname, st = dsem, 1
        elif dma:
            assert dsem, "dma needs dsem"
            dsem = self.SEM_ALIAS.get(dsem, dsem)
            self.getsem(dsem)
            sname, st = dsem, 16
        else:
            sname, st = ESEM[eng], 1
        if sem is not None:
            sname = sem
        if step is not None:
            st = step
        if dma or cc:
            deps = list(deps) + [(sname, self.cnt.get(sname, 0))]
        best = {}
        for tk in deps:
            if tk is None:
                continue
            n, v = tk
            if v > best.get(n, 0):
                best[n] = v
        seen = self.waited[eng]
        waits = []
        for n, v in best.items():
            if v > seen.get(n, 0):
                waits.append((n, v))
                seen[n] = v
        self.cnt[sname] += st
        tick = (sname, self.cnt[sname])

        def fn(e, waits=tuple(waits), build=build, sname=sname, st=st):
            for n, v in waits:
                e.wait_ge(self.sem[n], v)
            inst = build(e)
            inst.then_inc(self.sem[sname], st)
        self.q[eng].append(fn)
        return tick

    def wait_only(self, eng, deps):
        best = {}
        for tk in deps:
            if tk is None:
                continue
            n, v = tk
            if v > best.get(n, 0):
                best[n] = v
        seen = self.waited[eng]
        waits = []
        for n, v in best.items():
            if v > seen.get(n, 0):
                waits.append((n, v))
                seen[n] = v
        if not waits:
            return

        def fn(e, waits=tuple(waits)):
            for n, v in waits:
                e.wait_ge(self.sem[n], v)
        self.q[eng].append(fn)

    def now(self, *names):
        names = [self.SEM_ALIAS.get(n, n) for n in names]
        return [(n, self.cnt.get(n, 0)) for n in names]

    def realize(self, block):
        for eng in ("sync", "gpsimd", "tensor", "vector", "scalar"):
            ops = self.q[eng]

            def body(e, ops=ops):
                for fn in ops:
                    fn(e)
            getattr(block, eng)(body)



def prep_core_inputs(inputs, cfg, core):
    S, T = cfg.S, cfg.T
    b, r = core // 4, core % 4
    hlo, hhi = 4 * r * HD, (4 * r + 4) * HD
    W = lambda k: np.ascontiguousarray(np.asarray(inputs[k]), dtype=np.float32)
    g = {}
    x = W("L0_attn_x")
    g["x_in"] = np.ascontiguousarray(x[b, r * T:(r + 1) * T, :])
    g["idn"] = np.eye(128, dtype=np.float32)
    hs = np.zeros((NH, HPC), dtype=np.float32)
    for i in range(HPC):
        hs[4 * r + i, i] = 1.0
    g["head_sel"] = hs
    g["r0mask"] = np.full((1, 1), 1.0 if r == 0 else 0.0, dtype=np.float32)
    if 0 in cfg.layers:
        g["l0_wqT"] = np.ascontiguousarray(W("L0_attn_wq")[hlo:hhi, :].T)
        g["l0_wkT"] = np.ascontiguousarray(W("L0_attn_wk")[hlo:hhi, :].T)
        g["l0_wvT"] = np.ascontiguousarray(W("L0_attn_wv")[hlo:hhi, :].T)
        g["l0_bq"] = W("L0_attn_bq")[hlo:hhi].reshape(1, 256)
        g["l0_bk"] = W("L0_attn_bk")[hlo:hhi].reshape(1, 256)
        g["l0_bv"] = W("L0_attn_bv")[hlo:hhi].reshape(1, 256)
        g["l0_woT"] = np.ascontiguousarray(W("L0_attn_wo").T[hlo:hhi, :])
        g["l0_bo"] = W("L0_attn_bo").reshape(1, D)
        g["l0_sw1T"] = np.ascontiguousarray(W("L0_attn_sw1").T)
        g["l0_sb1"] = W("L0_attn_sb1").reshape(1, 512)
        g["l0_sw2T"] = np.ascontiguousarray(W("L0_attn_sw2").T)
        g["l0_sb2"] = W("L0_attn_sb2").reshape(1, 3)
        g["l0_pw1T"] = np.ascontiguousarray(np.stack([W("L0_attn_pw1")[p].T for p in range(3)]))
        g["l0_pb1"] = np.ascontiguousarray(W("L0_attn_pb1").reshape(3, 1, 512))
        g["l0_pw2T"] = np.ascontiguousarray(np.stack([W("L0_attn_pw2")[p].T for p in range(3)]))
        g["l0_pb2"] = np.ascontiguousarray(W("L0_attn_pb2").reshape(3, 1, 16))
        g["l0_kern_sl"] = np.ascontiguousarray(W("L0_ffn_kernel")[128 * core:128 * (core + 1), :])
        g["l0_fbias"] = W("L0_ffn_bias").reshape(1, D)
        g["l0_a"] = W("L0_ffn_a")
        g["l0_b"] = W("L0_ffn_b")
    if 1 in cfg.layers:
        NQ = cfg.NQ
        g["l1_wqT"] = np.ascontiguousarray(W("L1_attn_W_q")[hlo:hhi, :].T)
        g["l1_wkT"] = np.ascontiguousarray(W("L1_attn_W_k")[hlo:hhi, :].T)
        g["l1_wvT"] = np.ascontiguousarray(W("L1_attn_W_v")[hlo:hhi, :].T)
        g["l1_woT"] = np.ascontiguousarray(W("L1_attn_W_o").T[hlo:hhi, :])
        g["l1_gnw"] = np.ascontiguousarray(W("L1_attn_gn_weight")[hlo:hhi].reshape(2, 128).T)
        g["l1_gnb"] = np.ascontiguousarray(W("L1_attn_gn_bias")[hlo:hhi].reshape(2, 128).T)
        gam = (1.0 - 2.0 ** (-5.0 - np.arange(NH, dtype=np.float64)))
        qd = np.zeros((128, 2, 512), dtype=np.float64)
        for c in range(2):
            for half in range(2):
                h = 4 * r + 2 * c + half
                qd[64 * half:64 * half + 64, c, :] = gam[h] ** np.arange(512)[None, :]
        g["l1_qd"] = qd.astype(np.float32)
        ncol = 4 * sum(4 * qt + 4 for qt in range(NQ))
        cpt = np.zeros((128, ncol), dtype=np.float64)
        p_idx = np.arange(128)
        col = 0
        for qt in range(NQ):
            for hl in range(4):
                h = 4 * r + hl
                for kc in range(4 * qt + 4):
                    e_ = 512.0 * qt - kc * 128.0 - p_idx
                    cpt[:, col] = gam[h] ** e_ / 8.0
                    col += 1
        g["l1_cpt"] = cpt.astype(np.float32)
        g["l1_wg"] = W("L1_ffn_w_gate")
        g["l1_wu"] = W("L1_ffn_w_up")
        g["l1_wd"] = W("L1_ffn_w_down")
        hsl = np.zeros((128, 2), dtype=np.float32)
        hsl[0:64, 0] = 1.0
        hsl[64:128, 1] = 1.0
        g["halfsel"] = hsl
        g["halfselT"] = np.ascontiguousarray(hsl.T)
    if 2 in cfg.layers:
        wqkv = W("L2_attn_w_qkv")
        bqkv = W("L2_attn_b_qkv")
        g["l2_wq"] = np.ascontiguousarray(wqkv[:, hlo:hhi])
        g["l2_wk"] = np.ascontiguousarray(wqkv[:, 1024 + hlo:1024 + hhi])
        g["l2_wv"] = np.ascontiguousarray(wqkv[:, 2048 + hlo:2048 + hhi])
        g["l2_bq"] = bqkv[hlo:hhi].reshape(1, 256)
        g["l2_bk"] = bqkv[1024 + hlo:1024 + hhi].reshape(1, 256)
        g["l2_bv"] = bqkv[2048 + hlo:2048 + hhi].reshape(1, 256)
        g["l2_wproj"] = np.ascontiguousarray(W("L2_attn_w_proj")[hlo:hhi, :])
        g["l2_bproj"] = W("L2_attn_b_proj").reshape(1, D)
        g["l2_wexpT_sl"] = np.ascontiguousarray(W("L2_ffn_W_exp").T[:, 128 * core:128 * (core + 1)])
        g["l2_wcon_sl"] = np.ascontiguousarray(W("L2_ffn_W_con")[:, 128 * core:128 * (core + 1)])
        up_w = W("L2_ffn_up_weights")
        dn_w = W("L2_ffn_down_weights")
        def ab_tables(wst, transposed):
            A = np.zeros((12, 128, 32), dtype=np.float32)
            Bt = np.zeros((12, 128, 32), dtype=np.float32)
            for ent in range(12):
                stg = (11 - ent) if transposed else ent
                w = wst[stg].reshape(-1, 2, 2)
                s = 2 ** stg
                for i in range(4096):
                    blk, half, off = i // (2 * s), (i // s) % 2, i % s
                    k = blk * s + off
                    if half == 0:
                        a_, b_ = (w[k, 0, 0], w[k, 0, 1]) if not transposed else (w[k, 0, 0], w[k, 1, 0])
                    else:
                        a_, b_ = (w[k, 1, 1], w[k, 1, 0]) if not transposed else (w[k, 1, 1], w[k, 0, 1])
                    A[ent, i % 128, i // 128] = a_
                    Bt[ent, i % 128, i // 128] = b_
            return A, Bt
        g["l2_up_a"], g["l2_up_b"] = ab_tables(up_w, False)
        g["l2_dn_a"], g["l2_dn_b"] = ab_tables(dn_w, True)
        P = np.zeros((7, 128, 128), dtype=np.float32)
        for stg in range(7):
            s = 2 ** stg
            for i in range(128):
                P[stg, i ^ s, i] = 1.0
        g["l2_perm"] = P
        be = np.zeros((4096, 4), dtype=np.float32)
        be[:, 0] = W("L2_ffn_b_exp")
        g["l2_bexp"] = be
        g["l2_upb"] = np.ascontiguousarray(W("L2_ffn_up_bias").reshape(32, 128).T)
        g["l2_bcon"] = W("L2_ffn_b_con").reshape(1, D)
        g["l2_dnb"] = W("L2_ffn_down_bias").reshape(1, D)
    if 3 in cfg.layers:
        g["l3_n1w"] = W("L3_attn_norm1_w").reshape(1, D)
        g["l3_n2w"] = W("L3_attn_norm2_w").reshape(1, D)
        g["l3_gwq"] = np.ascontiguousarray(W("L3_attn_g_wq")[:, hlo:hhi])
        g["l3_gkvd"] = W("L3_attn_g_kv_down")
        g["l3_gkvup"] = np.ascontiguousarray(np.concatenate(
            [W("L3_attn_g_k_up")[:, r * 64:(r + 1) * 64],
             W("L3_attn_g_v_up")[:, r * 64:(r + 1) * 64]], axis=1))
        g["l3_gwo"] = np.ascontiguousarray(W("L3_attn_g_wo")[hlo:hhi, :])
        g["l3_lwq"] = np.ascontiguousarray(W("L3_attn_l_wq")[:, hlo:hhi])
        g["l3_lkv"] = np.ascontiguousarray(np.concatenate(
            [W("L3_attn_l_wk")[:, r * 64:(r + 1) * 64],
             W("L3_attn_l_wv")[:, r * 64:(r + 1) * 64]], axis=1))
        g["l3_lwo"] = np.ascontiguousarray(W("L3_attn_l_wo")[hlo:hhi, :])
        g["l3_gatewT"] = np.ascontiguousarray(W("L3_ffn_gate_w").T)
        g["l3_wg"] = W("L3_ffn_wg")
        g["l3_wu"] = W("L3_ffn_wu")
        g["l3_wd"] = W("L3_ffn_wd")
        g["l3_zb"] = np.zeros((1, D), np.float32)
        for base, tag in ((1e6, "g"), (1e4, "l")):
            inv = 1.0 / base ** (np.arange(0, 64, 2) / 64.0)
            fr = np.arange(S)[:, None] * inv[None, :]
            c4 = np.tile(np.cos(fr), (1, 4)).astype(np.float32)
            s4 = np.tile(np.sin(fr), (1, 4)).astype(np.float32)
            pk = np.concatenate([c4, s4], axis=1)
            g[f"l3_rope{tag}"] = np.ascontiguousarray(
                pk.reshape(S // 128, 128, 256).transpose(1, 0, 2))
    return g


def input_specs_for(cfg):
    T = cfg.T
    sp = {"x_in": ([T, D], F32), "idn": ([128, 128], F32), "head_sel": ([NH, HPC], F32),
          "r0mask": ([1, 1], F32)}
    if 0 in cfg.layers:
        sp.update({
            "l0_wqT": ([D, 256], F32), "l0_wkT": ([D, 256], F32), "l0_wvT": ([D, 256], F32),
            "l0_bq": ([1, 256], F32), "l0_bk": ([1, 256], F32), "l0_bv": ([1, 256], F32),
            "l0_woT": ([256, D], F32), "l0_bo": ([1, D], F32),
            "l0_sw1T": ([D, 512], F32), "l0_sb1": ([1, 512], F32),
            "l0_sw2T": ([512, 3], F32), "l0_sb2": ([1, 3], F32),
            "l0_pw1T": ([3, D, 512], F32), "l0_pb1": ([3, 1, 512], F32),
            "l0_pw2T": ([3, 512, 16], F32), "l0_pb2": ([3, 1, 16], F32),
            "l0_kern_sl": ([128, D], F32), "l0_fbias": ([1, D], F32),
            "l0_a": ([D, 16], F32), "l0_b": ([16, D], F32),
        })
    if 1 in cfg.layers:
        ncol = 4 * sum(4 * qt + 4 for qt in range(cfg.NQ))
        sp.update({
            "l1_wqT": ([D, 256], F32), "l1_wkT": ([D, 256], F32), "l1_wvT": ([D, 256], F32),
            "l1_woT": ([256, D], F32),
            "l1_gnw": ([128, 2], F32), "l1_gnb": ([128, 2], F32),
            "l1_qd": ([128, 2, 512], F32), "l1_cpt": ([128, ncol], F32),
            "l1_wg": ([2816, D], F32), "l1_wu": ([2816, D], F32), "l1_wd": ([D, 2816], F32),
            "halfsel": ([128, 2], F32), "halfselT": ([2, 128], F32),
        })
    if 2 in cfg.layers:
        sp.update({
            "l2_wq": ([D, 256], F32), "l2_wk": ([D, 256], F32), "l2_wv": ([D, 256], F32),
            "l2_bq": ([1, 256], F32), "l2_bk": ([1, 256], F32), "l2_bv": ([1, 256], F32),
            "l2_wproj": ([256, D], F32), "l2_bproj": ([1, D], F32),
            "l2_wexpT_sl": ([4096, 128], F32), "l2_wcon_sl": ([4096, 128], F32),
            "l2_up_a": ([12, 128, 32], F32), "l2_up_b": ([12, 128, 32], F32),
            "l2_dn_a": ([12, 128, 32], F32), "l2_dn_b": ([12, 128, 32], F32),
            "l2_perm": ([7, 128, 128], F32),
            "l2_bexp": ([4096, 4], F32), "l2_upb": ([128, 32], F32),
            "l2_bcon": ([1, D], F32), "l2_dnb": ([1, D], F32),
        })
    if 3 in cfg.layers:
        S = cfg.S
        sp.update({
            "l3_n1w": ([1, D], F32), "l3_n2w": ([1, D], F32),
            "l3_gwq": ([D, 256], F32), "l3_gkvd": ([D, 256], F32),
            "l3_gkvup": ([256, 128], F32), "l3_gwo": ([256, D], F32),
            "l3_lwq": ([D, 256], F32), "l3_lkv": ([D, 128], F32),
            "l3_lwo": ([256, D], F32),
            "l3_gatewT": ([D, 8], F32),
            "l3_wg": ([8, D, 256], F32), "l3_wu": ([8, D, 256], F32),
            "l3_wd": ([8, 256, D], F32),
            "l3_zb": ([1, D], F32),
            "l3_ropeg": ([128, S // 128, 256], F32),
            "l3_ropel": ([128, S // 128, 256], F32),
        })
    return sp



def dma(ctx, eng, out_ap, in_ap, deps=(), slow=False, dsem=None):
    return ctx.op(eng, lambda e, o=out_ap, i=in_ap, s=slow: e.dma_start(
        out=o, in_=i, allow_slow_non_contiguous=s), deps=deps, dma=True, dsem=dsem)


def dma_acc(ctx, out_ap, in_ap, deps=(), dsem=None):
    return ctx.op("gpsimd",
                  lambda e, o=out_ap, i=in_ap: e.dma_start(out=o, in_=i, accum_op=ALU.add),
                  deps=deps, dma=True, dsem=dsem)


def emit_init(ctx):
    t = ctx.t
    ctx.op("vector", lambda e: e.memset(t["q_pad"][:].rearrange("p a b -> p (a b)"), 0.0))
    ctx.op("vector", lambda e: e.memset(t["ones1"][:], 1.0))
    ctx.op("vector", lambda e: e.memset(t["ones1b"][:], 1.0))
    ctx.op("vector", lambda e: e.memset(t["ones_row"][:], 1.0))
    ctx.op("vector", lambda e: e.memset(t["zero_c"][:], 0.0))
    t["init_dve"] = ctx.op("vector", lambda e: e.memset(t["bias_c"][:], EXP_BIAS))
    t["vones"] = ctx.op("vector", lambda e: e.memset(
        t["v_aug"][:, :, :, 64:65], 1.0))
    t["idn_ld"] = dma(ctx, "gpsimd", t["idn_bf"][:], ctx.dram["idn"], dsem="sm_idn")
    t["idnf_ld"] = dma(ctx, "sync", t["idn_f32"][:], ctx.dram["idn"], dsem="sm_idnf")
    t["hsel_ld"] = dma(ctx, "sync", t["hsel"][:], ctx.dram["head_sel"], dsem="sm_hsel")
    t["r0m_ld"] = dma(ctx, "sync", t["r0mask_sb"][:], ctx.dram["r0mask"], dsem="sm_r0m")


def emit_norm(ctx, x_src, kind="ln", w_row_ap=None, deps=(), store32=None,
              war_deps=()):
    cfg, t = ctx.cfg, ctx.t
    TT = cfg.TT
    xt, sq, st = t["x_stage"], t["scr_f32"], t["stats"]
    xn_tm, xn_fm = t["xn_tm"], t["xn_fm"]
    trp = [t["ps_s00"][:].bitcast(BF16), t["ps_s10"][:].bitcast(BF16)]
    tm_t, fm_t = [None] * TT, [None] * TT
    s32_t = []

    wbc_t = None
    if w_row_ap is not None:
        wl = dma(ctx, "sync", t["wrow"][:], w_row_ap, deps=deps, dsem="sm_wrow")
        m1 = ctx.op("tensor", lambda e: e.matmul(t["ps_a"][:], t["ones1"][:], t["wrow"][:, 0:512],
                                                 start=True, stop=True),
                    deps=[wl, t["init_dve"], t.get("psab_free")])
        m2 = ctx.op("tensor", lambda e: e.matmul(t["ps_b"][:], t["ones1"][:], t["wrow"][:, 512:1024],
                                                 start=True, stop=True), deps=[wl])
        c1 = ctx.op("vector", lambda e: e.tensor_copy(t["wbc_f32"][:, 0:512], t["ps_a"][:]), deps=[m1])
        wbc_t = ctx.op("vector", lambda e: e.tensor_copy(t["wbc_f32"][:, 512:1024], t["ps_b"][:]), deps=[m2])
        t["psab_free"] = wbc_t

    ld = [None] * TT
    prev_ev = {}
    xt_free = {}
    tr_last = {}
    xt_last = {}
    for i in range(TT):
        ld[i] = dma(ctx, "sync", xt[:, i % 2, :], x_src[i * 128:(i + 1) * 128, :],
                    deps=list(deps) + [t.get("xstage_free"), xt_free.get(i - 2),
                                       xt_last.get(i - 2),
                                       tm_t[i - 2] if i >= 2 else None],
                    dsem=f"sm_xst{i % 2}")
        sqt = ctx.op("vector", lambda e, i=i: e.tensor_tensor(
            out=sq[:], in0=xt[:, i % 2, :], in1=xt[:, i % 2, :], op=ALU.mult),
            deps=[ld[i], tm_t[i - 1] if i >= 1 else None, t.get("_s32d")])
        r1 = ctx.op("vector", lambda e, i=i: e.tensor_reduce(st[:, 0:1], xt[:, i % 2, :], AX.X, ALU.add),
                    deps=[ld[i]])
        r2 = ctx.op("vector", lambda e: e.tensor_reduce(st[:, 1:2], sq[:], AX.X, ALU.add),
                    deps=[sqt])
        if kind == "ln":
            a1 = ctx.op("vector", lambda e: e.tensor_scalar(out=st[:, 2:3], in0=st[:, 0:1],
                                                            scalar1=1.0 / D, scalar2=None, op0=ALU.mult), deps=[r1])
            a2 = ctx.op("vector", lambda e: e.tensor_scalar(out=st[:, 3:4], in0=st[:, 1:2],
                                                            scalar1=1.0 / D, scalar2=None, op0=ALU.mult), deps=[r2])
            a3 = ctx.op("vector", lambda e: e.tensor_tensor(out=st[:, 4:5], in0=st[:, 2:3],
                                                            in1=st[:, 2:3], op=ALU.mult), deps=[a1])
            a4 = ctx.op("vector", lambda e: e.tensor_tensor(out=st[:, 3:4], in0=st[:, 3:4],
                                                            in1=st[:, 4:5], op=ALU.subtract), deps=[a2, a3])
            eps = LN_EPS
            mean_dep = a1
        else:
            mean_dep = ctx.op("vector", lambda e: e.memset(st[:, 2:3], 0.0), deps=[])
            a4 = ctx.op("vector", lambda e: e.tensor_scalar(out=st[:, 3:4], in0=st[:, 1:2],
                                                            scalar1=1.0 / D, scalar2=None, op0=ALU.mult), deps=[r2])
            eps = RMS_EPS
        a5 = ctx.op("vector", lambda e, eps=eps: e.tensor_scalar(
            out=st[:, 3:4], in0=st[:, 3:4], scalar1=eps, scalar2=None, op0=ALU.add), deps=[a4])
        a6 = ctx.op("vector", lambda e: e.reciprocal(st[:, 4:5], st[:, 3:4]), deps=[a5])
        a7 = ctx.op("scalar", lambda e: e.activation(st[:, 6:7], st[:, 4:5], AF.Sqrt), deps=[a6])
        if w_row_ap is None:
            tm_t[i] = ctx.op("vector", lambda e, i=i: e.tensor_scalar(
                out=xn_tm[:, i % 2, :], in0=xt[:, i % 2, :], scalar1=st[:, 2:3], scalar2=st[:, 6:7],
                op0=ALU.subtract, op1=ALU.mult), deps=[a7, mean_dep, tr_last.get(i - 2)])
            if store32 is not None:
                s32 = ctx.op("vector", lambda e, i=i: e.tensor_scalar(
                    out=sq[:], in0=xt[:, i % 2, :], scalar1=st[:, 2:3], scalar2=st[:, 6:7],
                    op0=ALU.subtract, op1=ALU.mult), deps=[tm_t[i], t.get("_s32d")])
                t["_s32d"] = ctx.op("gpsimd", lambda e, i=i: e.dma_start(
                    out=ctx.dram[store32][i * 128:(i + 1) * 128, :], in_=sq[:]),
                    deps=[s32], dma=True, dsem="sm_xn32")
                s32_t.append(t["_s32d"])
                xt_last[i] = s32
        else:
            u = ctx.op("vector", lambda e, i=i: e.tensor_scalar(
                out=sq[:], in0=xt[:, i % 2, :], scalar1=st[:, 2:3], scalar2=st[:, 6:7],
                op0=ALU.subtract, op1=ALU.mult), deps=[a7, mean_dep, wbc_t])
            tm_t[i] = ctx.op("vector", lambda e, i=i: e.tensor_tensor(
                out=xn_tm[:, i % 2, :], in0=sq[:], in1=t["wbc_f32"][:], op=ALU.mult),
                deps=[u, tr_last.get(i - 2)])
        buf = trp[i % 2]
        tdeps = [tm_t[i], t["idn_ld"], prev_ev.get(i - 2), t.get("sbank_free")] + (
            list(war_deps) if i == 0 else [])
        last = None
        for f in range(8):
            last = ctx.op("tensor", lambda e, i=i, f=f, buf=buf: e.transpose(
                buf[:, f * 128:(f + 1) * 128], xn_tm[:, i % 2, f * 128:(f + 1) * 128],
                t["idn_bf"][:]), deps=tdeps if f == 0 else [])
        tr_last[i] = last
        fm_t[i] = ctx.op("vector", lambda e, i=i, buf=buf: e.tensor_copy(
            xn_fm[:, :, i * 128:(i + 1) * 128], buf[:].rearrange("p (a b) -> p a b", a=8)),
            deps=[last])
        prev_ev[i] = fm_t[i]
        xt_free[i] = tm_t[i]
    t["xstage_free"] = xt_last.get(TT - 1, tm_t[-1])
    return {"tm": tm_t, "fm": fm_t, "ld": ld, "s32": s32_t}


def emit_allgather_xn(ctx, tag, deps):
    t = ctx.t
    agi, ago = ctx.dram[tag + "_in"], ctx.dram[tag + "_out"]
    st = dma(ctx, "gpsimd", agi.rearrange("(a p) t -> p a t", p=128), t["xn_fm"][:],
             deps=list(deps) + [t.get("ag_free")], dsem="sm_agst")
    cc = ctx.op("gpsimd", lambda e: e.collective_compute(
        "AllGather", ALU.bypass, replica_groups=GROUPS4, ins=[agi.opt()], outs=[ago.opt()]),
        deps=[st] + ctx.now("sm_xnstrm0"), cc=True, dsem="cc_ag")
    t["ag_free"] = cc
    return cc


def emit_qkv(ctx, wq_name, wk_name, wv_name, bq, bk, bv, ag_tag, ag_done,
             q_scale_tiles=None, q_scale_dep=None):
    cfg, t, d = ctx.cfg, ctx.t, ctx.dram
    S, T, NQ = cfg.S, cfg.T, cfg.NQ
    ago = d[ag_tag + "_out"]
    wq, wk, wv = t["w_q"], t["w_k"], t["w_v"]
    lw = [dma(ctx, "gpsimd", wq[:], d[wq_name].rearrange("(a p) m -> p a m", p=128),
              deps=[t.get("wqkv_free")], dsem="sm_wq"),
          dma(ctx, "gpsimd", wk[:], d[wk_name].rearrange("(a p) m -> p a m", p=128), dsem="sm_wk"),
          dma(ctx, "gpsimd", wv[:], d[wv_name].rearrange("(a p) m -> p a m", p=128), dsem="sm_wv")]
    lb = [dma(ctx, "gpsimd", t["b_q"][:], d[bq], dsem="sm_bq") if bq else None,
          dma(ctx, "gpsimd", t["b_k"][:], d[bk], dsem="sm_bk") if bk else None,
          dma(ctx, "gpsimd", t["b_v"][:], d[bv], dsem="sm_bv") if bv else None]
    has_bias = bq is not None
    xs = t["xn_strm"]
    pa, pb = t["ps_a"], t["ps_b"]
    last = None
    evq = {}
    for qt in range(NQ):
        j = (qt * 512) // T
        lc = qt * 512 - j * T
        buf = 0
        ldx = dma(ctx, "sync", xs[:, buf, :, :],
                  ago[j * D:(j + 1) * D, lc:lc + 512].rearrange("(a p) t -> p a t", p=128),
                  deps=[ag_done, evq.get(qt - 1)], dsem="sm_xnstrm0")
        for which, wt, bt in (("q", wq, t["b_q"]), ("k", wk, t["b_k"])):
            for mh in (0, 1):
                ps = pa if mh == 0 else pb
                deps0 = [ldx, lw[0], lw[1], lb[0], lb[1], t.get("psab_free"), t["init_dve"]]
                if which == "q":
                    deps0.append(t.get("qstg_free"))
                mmlast = None
                for kc in range(8):
                    mmlast = ctx.op("tensor", lambda e, wt=wt, mh=mh, kc=kc, ps=ps, buf=buf: e.matmul(
                        ps[:], wt[:, kc, mh * 128:(mh + 1) * 128], xs[:, buf, kc, :],
                        start=(kc == 0), stop=(kc == 7 and not has_bias)),
                        deps=deps0 if kc == 0 else [])
                if has_bias:
                    mmlast = ctx.op("tensor", lambda e, bt=bt, mh=mh, ps=ps: e.matmul(
                        ps[:], bt[:, mh * 128:(mh + 1) * 128], t["ones_row"][:],
                        start=False, stop=True), deps=[])
                if which == "q":
                    if q_scale_tiles is not None:
                        ev = ctx.op("vector", lambda e, mh=mh, ps=ps: e.tensor_tensor(
                            out=t["q_stg"][:], in0=ps[:], in1=q_scale_tiles[mh],
                            op=ALU.mult), deps=[mmlast, t.get("qstg_free"), q_scale_dep])
                    else:
                        ev = ctx.op("vector", lambda e, mh=mh, ps=ps: e.tensor_copy(
                            t["q_stg"][:], ps[:]), deps=[mmlast, t.get("qstg_free")])
                    qd = ctx.op("scalar", lambda e, mh=mh, qt=qt: e.dma_start(
                        out=ctx.dram["q_scr"][mh * 128:(mh + 1) * 128, qt * 512:(qt + 1) * 512],
                        in_=t["q_stg"][:]), deps=[ev], dma=True, dsem="sm_qscr")
                    t["qstg_free"] = qd
                    t["qscr_last"] = qd
                else:
                    ev = ctx.op("vector", lambda e, mh=mh, qt=qt, ps=ps: e.tensor_copy(
                        t["kt_fm"][:, mh, qt * 512:(qt + 1) * 512], ps[:]), deps=[mmlast])
                t["psab_free"] = ev
        for tc in range(4):
            kcg = qt * 4 + tc
            ps = pa if tc % 2 == 0 else pb
            deps0 = [lw[2], lb[2], t.get("psab_free"), t["vones"]]
            mmlast = None
            for kc in range(8):
                mmlast = ctx.op("tensor", lambda e, kc=kc, tc=tc, ps=ps, buf=buf: e.matmul(
                    ps[:, 0:256], xs[:, buf, kc, tc * 128:(tc + 1) * 128], wv[:, kc, :],
                    start=(kc == 0), stop=(kc == 7 and not has_bias)),
                    deps=deps0 if kc == 0 else [])
            if has_bias:
                mmlast = ctx.op("tensor", lambda e, ps=ps: e.matmul(
                    ps[:, 0:256], t["ones1b"][:], t["b_v"][:],
                    start=False, stop=True), deps=[])
            ev = ctx.op("vector", lambda e, kcg=kcg, ps=ps: e.tensor_copy(
                t["v_aug"][:, kcg, :, 0:64], ps[:, 0:256].rearrange("p (a b) -> p a b", a=4)),
                deps=[mmlast])
            t["psab_free"] = ev
            last = ev
        evq[qt] = last
    return last


def wo_window(ctx, qt, o2_deps, lwo, bo_bf, ydma, yps):
    t = ctx.t
    yp = ctx.dram["y_part"]
    for nb in (0, 1):
        dep0 = list(o2_deps) + [lwo, bo_bf, ydma.get((qt - 1, nb))]
        prev_ev = None
        for tc in range(4):
            mm = None
            for hp in (0, 1):
                d0 = (dep0 if tc == 0 else [prev_ev]) if hp == 0 else []
                mm = ctx.op("tensor", lambda e, tc=tc, hp=hp, nb=nb: e.matmul(
                    yps[nb][:], t["o2"][:, hp, tc * 128:(tc + 1) * 128],
                    t["wo_sl"][:, hp, nb * 512:(nb + 1) * 512],
                    start=(hp == 0), stop=False), deps=d0)
            mm = ctx.op("tensor", lambda e, nb=nb: e.matmul(
                yps[nb][:], t["ones1b"][:], t["borow_bf"][:, nb * 512:(nb + 1) * 512],
                start=False, stop=True), deps=[])
            ev = ctx.op("vector", lambda e, nb=nb: e.tensor_copy(
                t["y_bf"][:, nb, :], yps[nb][:]), deps=[mm, ydma.get((qt, nb, tc - 1))])
            prev_ev = ev
            yd = ctx.op("scalar", lambda e, qt=qt, tc=tc, nb=nb: e.dma_start(
                out=yp[qt * 512 + tc * 128:qt * 512 + (tc + 1) * 128,
                       nb * 512:(nb + 1) * 512],
                in_=t["y_bf"][:, nb, :]), deps=[ev], dma=True, dsem=f"sm_ybf{nb}")
            ydma[(qt, nb, tc)] = yd
            ydma[(qt, nb)] = yd
    return ctx.now("pe")[0]


def emit_attention(ctx, mode, hg=False, wo_name=None, bo_name=None):
    cfg, t = ctx.cfg, ctx.t
    S, T, NQ = cfg.S, cfg.T, cfg.NQ
    NKC = S // 128
    sps = [[t["ps_s00"], t["ps_s01"]], [t["ps_s10"], t["ps_s11"]]]
    ops = [t["ps_o0"], t["ps_o1"]]
    yps = [t["ps_a"], t["ps_b"]]
    causal = mode in ("l0", "l3g", "l1")
    softmax = mode != "l1"
    yp = ctx.dram["y_part"]

    if softmax:
        lwo = dma(ctx, "gpsimd", t["wo_sl"][:],
                  ctx.dram[wo_name].rearrange("(a p) m -> p a m", p=128),
                  deps=[t.get("wo_sl_free")], dsem="sm_wosl")
        lbo = dma(ctx, "gpsimd", t["borow_bf"][:], ctx.dram[bo_name], dsem="sm_borow")
        lmk = t["r0m_ld"]
        bo_bf = ctx.op("vector", lambda e: e.tensor_scalar(
            out=t["borow_bf"][:], in0=t["borow_bf"][:], scalar1=t["r0mask_sb"][0:1, 0:1],
            scalar2=None, op0=ALU.mult), deps=[lbo, lmk])
    else:
        lwo = bo_bf = None
        gz = ctx.op("vector", lambda e: e.memset(t["gn_acc"][:], 0.0))
        t["gn_acc_zero"] = gz

    def kc_range(qt):
        if causal:
            return list(range(0, 4 * qt + 4))
        if mode == "l2":
            return list(range(NKC))
        if mode == "l3l":
            return list(range(max(0, (qt * 512 - 255) // 128), 4 * qt + 4))
        raise ValueError(mode)

    qpad_free = [None, None]
    prev_exp = {}
    prev_av = {}
    o_free = [None, None]
    bc_free = [None, None]
    ydma = {}
    ydma[(-1, 0)] = ydma[(-1, 1)] = t.get("ypart_free")
    cpt_off = 0
    cpt_ld = {}
    ysc_t = {}
    for qt in range(NQ):
        kcs = kc_range(qt)
        n = len(kcs)
        o2_t = {}
        if not softmax:
            csl = qt % 2
            cpt_ld[qt] = ctx.op("sync", lambda e, qt=qt, csl=csl, n=n, off=cpt_off: e.dma_start(
                out=t["cpt_sb"][:, csl, 0:4 * n],
                in_=ctx.dram["l1_cpt"][:, off:off + 4 * n]),
                deps=[cpt_ld.get(qt - 2), prev_exp.get((1, 0)), prev_exp.get((1, 1)),
                      t["qkv_done"] if qt == 0 else None,
                      t.get("l0_done") if qt == 0 else None],
                dma=True, dsem=f"sm_cpt{csl}")
            cpt_off += 4 * n
        for hp in (0, 1):
            qk_t, exp_t, av_t, msk_t = {}, {}, {}, {}
            s0 = ctx.op("sync", lambda e, hp=hp, qt=qt: e.dma_start(
                out=t["q_pad"][0:64, 0, :],
                in_=ctx.dram["q_scr"][hp * 128:hp * 128 + 64, qt * 512:(qt + 1) * 512]),
                deps=[t["qkv_done"], t["qscr_last"], t["init_dve"], qpad_free[0]],
                dma=True, dsem="sm_qpad0")
            s1 = ctx.op("sync", lambda e, hp=hp, qt=qt: e.dma_start(
                out=t["q_pad"][64:128, 1, :],
                in_=ctx.dram["q_scr"][hp * 128 + 64:hp * 128 + 128, qt * 512:(qt + 1) * 512]),
                deps=[qpad_free[1]], dma=True, dsem="sm_qpad1")
            qstg = [s0, s1]
            for gi in range(0, n, 2):
                grp = kcs[gi:gi + 2]
                for x in (0, 1):
                    dep0 = [qstg[x], exp_t.get((x, gi - 2)), exp_t.get((x, gi - 1)),
                            prev_exp.get((hp, x)) if gi == 0 else None,
                            bc_free[x] if gi == 0 else None,
                            t.get("trp_free") if (qt == 0 and hp == 0 and gi == 0) else None]
                    for bi, kc in enumerate(grp):
                        mm = ctx.op("tensor", lambda e, x=x, bi=bi, kc=kc, hp=hp: e.matmul(
                            sps[x][bi][:], t["kt_fm"][:, hp, kc * 128:(kc + 1) * 128],
                            t["q_pad"][:, x, :], start=True, stop=True),
                            deps=dep0 if bi == 0 else [])
                        qk_t[(x, gi + bi)] = mm
                for x in (0, 1):
                    for bi, kc in enumerate(grp):
                        h = 2 * hp + x
                        if softmax:
                            sc = t["hg_sc"][:, h:h + 1] if hg else 0.125
                            ex = ctx.op("scalar", lambda e, x=x, bi=bi, sc=sc: e.activation(
                                t["p_sb"][:, x, bi, :], sps[x][bi][:], AF.Exp,
                                scale=sc, bias=t["bias_c"][:]),
                                deps=[qk_t[(x, gi + bi)], av_t.get((x, gi - 2 + bi)),
                                      prev_av.get((hp, x)) if gi < 2 else None,
                                      t.get("hg_done") if hg else None])
                        else:
                            hl = 2 * hp + x
                            col = hl * n + gi + bi
                            csl = qt % 2
                            ex = ctx.op("scalar", lambda e, x=x, bi=bi, col=col, csl=csl: e.activation(
                                t["p_sb"][:, x, bi, :], sps[x][bi][:], AF.Copy,
                                scale=t["cpt_sb"][:, csl, col:col + 1], bias=0.0),
                                deps=[qk_t[(x, gi + bi)], av_t.get((x, gi - 2 + bi)),
                                      prev_av.get((hp, x)) if gi < 2 else None,
                                      cpt_ld[qt], t["init_dve"]])
                        exp_t[(x, gi + bi)] = ex
                for x in (0, 1):
                    for bi, kc in enumerate(grp):
                        dep_m = exp_t[(x, gi + bi)]
                        need_hi = (mode in ("l0", "l3g", "l3l", "l1")) and (kc * 128 + 127 > qt * 512)
                        need_lo = (mode == "l3l") and (kc * 128 <= qt * 512 + 511 - 256)
                        mt = None
                        tk0 = qt * 512 - kc * 128
                        if need_hi:
                            mt = ctx.op("gpsimd", lambda e, x=x, bi=bi, tk0=tk0: e.affine_select(
                                out=t["p_sb"][:, x, bi, :], in_=t["p_sb"][:, x, bi, :],
                                pattern=[[1, 512]], compare_op=ALU.is_ge, fill=0.0,
                                base=tk0, channel_multiplier=-1), deps=[dep_m])
                        if need_lo:
                            mt = ctx.op("gpsimd", lambda e, x=x, bi=bi, tk0=tk0: e.affine_select(
                                out=t["p_sb"][:, x, bi, :], in_=t["p_sb"][:, x, bi, :],
                                pattern=[[-1, 512]], compare_op=ALU.is_ge, fill=0.0,
                                base=255 - tk0, channel_multiplier=1), deps=[dep_m, mt])
                        msk_t[(x, gi + bi)] = mt
                for x in (0, 1):
                    h = 2 * hp + x
                    dep0 = [exp_t[(x, gi + len(grp) - 1)],
                            o_free[x] if gi == 0 else None] +                            [msk_t.get((x, gi + bi2)) for bi2 in range(len(grp))]
                    for bi, kc in enumerate(grp):
                        mm = ctx.op("tensor", lambda e, x=x, bi=bi, kc=kc, h=h, gi=gi, n=n: e.matmul(
                            ops[x][0:65, :], t["v_aug"][:, kc, h, :], t["p_sb"][:, x, bi, :],
                            start=(gi == 0 and bi == 0), stop=(gi + bi == n - 1)),
                            deps=dep0 if bi == 0 else [])
                        av_t[(x, gi + bi)] = mm
            qpad_free[0] = qk_t[(0, n - 1)]
            qpad_free[1] = qk_t[(1, n - 1)]
            if not softmax:
                for x in (0, 1):
                    cpf = ctx.op("vector", lambda e, x=x: e.tensor_copy(
                        t["evac"][0:64, x, :], ops[x][0:64, :]),
                        deps=[av_t[(x, n - 1)], o_free[x], t.get(("gnsq", hp, x)),
                              t["gn_acc_zero"], t.get("evac_free_l1")])
                    cp = ctx.op("vector", lambda e, x=x, hp=hp: e.tensor_copy(
                        t["o2"][x * 64:(x + 1) * 64, hp, :], ops[x][0:64, :]),
                        deps=[ysc_t.get((qt - 1, hp, x)), cpf])
                    yd = ctx.op("scalar", lambda e, x=x, hp=hp, qt=qt: e.dma_start(
                        out=ctx.dram["y_scr"][(2 * hp + x) * 64:(2 * hp + x + 1) * 64,
                                              qt * 512:(qt + 1) * 512],
                        in_=t["o2"][x * 64:(x + 1) * 64, hp, :]),
                        deps=[cp], dma=True, dsem=f"sm_ysc{hp}{x}")
                    ysc_t[(qt, hp, x)] = yd
                    r1 = ctx.op("vector", lambda e, x=x: e.tensor_reduce(
                        t["gn_tmp"][x * 64:(x + 1) * 64, 0:1], t["evac"][0:64, x, :],
                        AX.X, ALU.add), deps=[cpf])
                    o_free[x] = cp
                    a1 = ctx.op("vector", lambda e, x=x, hp=hp: e.tensor_tensor(
                        out=t["gn_acc"][x * 64:(x + 1) * 64, hp:hp + 1],
                        in0=t["gn_acc"][x * 64:(x + 1) * 64, hp:hp + 1],
                        in1=t["gn_tmp"][x * 64:(x + 1) * 64, 0:1], op=ALU.add), deps=[r1])
                    sqq = ctx.op("vector", lambda e, x=x: e.tensor_tensor(
                        out=t["evac"][0:64, x, :], in0=t["evac"][0:64, x, :],
                        in1=t["evac"][0:64, x, :], op=ALU.mult), deps=[r1])
                    r2 = ctx.op("vector", lambda e, x=x: e.tensor_reduce(
                        t["gn_tmp"][x * 64:(x + 1) * 64, 1:2],
                        t["evac"][0:64, x, :], AX.X, ALU.add), deps=[sqq])
                    t[("gnsq", hp, x)] = r2
                    a2 = ctx.op("vector", lambda e, x=x, hp=hp: e.tensor_tensor(
                        out=t["gn_acc"][x * 64:(x + 1) * 64, 2 + hp:3 + hp],
                        in0=t["gn_acc"][x * 64:(x + 1) * 64, 2 + hp:3 + hp],
                        in1=t["gn_tmp"][x * 64:(x + 1) * 64, 1:2], op=ALU.add), deps=[r2])
                    prev_exp[(hp, x)] = exp_t[(x, n - 1)]
                    prev_av[(hp, x)] = av_t[(x, n - 1)]
                    o2_t[(hp, x)] = a2
                continue
            for x in (0, 1):
                rc = ctx.op("vector", lambda e, x=x: e.reciprocal(
                    t["rinv"][:, x, :], ops[x][64:65, :]),
                    deps=[av_t[(x, n - 1)], o_free[x]])
                bc = ctx.op("tensor", lambda e, x=x: e.matmul(
                    sps[x][0][0:64, :], t["ones1"][:, 0:64], t["rinv"][:, x, :],
                    start=True, stop=True), deps=[rc, exp_t[(x, n - 1)]])
                cp = ctx.op("vector", lambda e, x=x: e.tensor_copy(
                    t["binv"][:, x, :], sps[x][0][0:64, :]), deps=[bc])
                bc_free[x] = cp
                ot = ctx.op("vector", lambda e, x=x, hp=hp: e.tensor_tensor(
                    out=t["o2"][x * 64:(x + 1) * 64, hp, :], in0=ops[x][0:64, :],
                    in1=t["binv"][:, x, :], op=ALU.mult),
                    deps=[cp, o2_t.get(("free", hp, x))])
                o_free[x] = ot
                prev_exp[(hp, x)] = exp_t[(x, n - 1)]
                prev_av[(hp, x)] = av_t[(x, n - 1)]
                o2_t[(hp, x)] = ot
        if softmax:
            lastwo = wo_window(ctx, qt, [o2_t[(0, 0)], o2_t[(0, 1)], o2_t[(1, 0)], o2_t[(1, 1)]],
                               lwo, bo_bf, ydma, yps)
            for hp in (0, 1):
                for x in (0, 1):
                    o2_t[("free", hp, x)] = lastwo
            t["o2_free"] = lastwo
    t["sbank_free"] = ctx.now("act")[0]
    t["wo_sl_free"] = ctx.now("pe")[0]
    if not softmax:
        return ctx.now("dve")[0]
    return [ydma[(qt, nb, tc)] for qt in range(NQ) for nb in (0, 1) for tc in range(4)]


def emit_rs_residual(ctx, x_dst, ydeps, seed_t, acc_deps=None):
    cfg, t, d = ctx.cfg, ctx.t, ctx.dram
    TT = cfg.TT
    cc = ctx.op("gpsimd", lambda e: e.collective_compute(
        "ReduceScatter", ALU.add, replica_groups=GROUPS4,
        ins=[d["y_part"].opt()], outs=[d["rs_out"].opt()]), deps=ydeps, cc=True,
        dsem="cc_rs")
    t["ypart_free"] = cc
    prev = {}
    prevcp = {}
    for tt in range(TT):
        buf = 0
        ldr = dma(ctx, "sync", t["rs_bf"][:, buf, :],
                  d["rs_out"][tt * 128:(tt + 1) * 128, :],
                  deps=[cc, prevcp.get(tt - 1),
                        t.get("rs_buf_free") if tt == 0 else None],
                  dsem="sm_rsbf0")
        cpr = ctx.op("vector", lambda e, buf=buf: e.tensor_copy(
            t["rs_f32"][:, 0, :], t["rs_bf"][:, buf, :]),
            deps=[ldr, prev.get(tt - 1)])
        prevcp[tt] = cpr
        if acc_deps is None:
            adeps = [t["seed_all"], t.get("ffn_acc_all0"), t.get("ffn_acc_all1")]
        else:
            adeps = list(acc_deps)
        acc = dma_acc(ctx, x_dst[tt * 128:(tt + 1) * 128, :], t["rs_f32"][:, 0, :],
                      deps=[cpr] + adeps, dsem=f"sm_rsacc{buf}")
        prev[tt] = acc
    t["rs_buf_free"] = prev[TT - 1]
    return [prev[tt] for tt in range(TT)]



def fp4_quant_slice(ctx, in_ap, out_ap, W, sl, dep):
    t = ctx.t
    Wb = W // 16
    B1 = t["x_stage"][:, 0, 0:W]
    B2 = t["x_stage"][:, 1, 0:W]
    B3 = t["scr_f32"][:, 0:W]
    B4 = t["scr_f32"][:, 512:512 + W]
    B5 = t["q4b"][:, 0:W]
    B6 = t["q4b2"][:, 0:W]
    mxf = t["xq_mx"][:].rearrange("p a b -> p (a b)")
    MX = mxf[:, 0:Wb]
    MXI = mxf[:, 176:176 + Wb]
    MX3 = MX.rearrange("p (b z) -> p b z", z=1)
    MXI3 = MXI.rearrange("p (b z) -> p b z", z=1)
    o = ctx.op
    v = "vector"
    d = o(v, lambda e: e.tensor_reduce(MX3,
                                       in_ap.rearrange("p (b k) -> p b k", k=16),
                                       AX.X, ALU.max, apply_absolute_value=True), deps=[dep])
    d = o(v, lambda e: e.tensor_scalar(out=MX, in0=MX, scalar1=1.0 / 6.0, scalar2=1e-12,
                                       op0=ALU.mult, op1=ALU.max), deps=[d])
    d = o(v, lambda e: e.reciprocal(MXI, MX), deps=[d])
    d = o(v, lambda e: e.tensor_tensor(
        out=B1.rearrange("p (b k) -> p b k", k=16), in0=in_ap.rearrange("p (b k) -> p b k", k=16),
        in1=MXI3.broadcast_to([128, Wb, 16]), op=ALU.mult), deps=[d])
    d = o(v, lambda e: e.tensor_tensor(out=B3, in0=B1, in1=B1, op=ALU.mult), deps=[d])
    d = o(v, lambda e: e.tensor_scalar(out=B2, in0=B1, scalar1=2.0, scalar2=RND_MAGIC,
                                       op0=ALU.mult, op1=ALU.add), deps=[d])
    d = o(v, lambda e: e.tensor_scalar(out=B2, in0=B2, scalar1=RND_MAGIC, scalar2=0.5,
                                       op0=ALU.subtract, op1=ALU.mult), deps=[d])
    d = o(v, lambda e: e.tensor_scalar(out=B4, in0=B1, scalar1=RND_MAGIC, scalar2=RND_MAGIC,
                                       op0=ALU.add, op1=ALU.subtract), deps=[d])
    d = o(v, lambda e: e.tensor_scalar(out=B5, in0=B1, scalar1=0.5, scalar2=RND_MAGIC,
                                       op0=ALU.mult, op1=ALU.add), deps=[d])
    d = o(v, lambda e: e.tensor_scalar(out=B5, in0=B5, scalar1=RND_MAGIC, scalar2=2.0,
                                       op0=ALU.subtract, op1=ALU.mult), deps=[d])
    d = o(v, lambda e: e.tensor_tensor(out=B5, in0=B5, in1=B4, op=ALU.subtract), deps=[d])
    d = o(v, lambda e: e.tensor_tensor(out=B4, in0=B4, in1=B2, op=ALU.subtract), deps=[d])
    d = o(v, lambda e: e.tensor_scalar(out=B6, in0=B3, scalar1=4.0, scalar2=None,
                                       op0=ALU.is_ge), deps=[d])
    d = o(v, lambda e: e.tensor_tensor(out=B4, in0=B4, in1=B6, op=ALU.mult), deps=[d])
    d = o(v, lambda e: e.tensor_tensor(out=B2, in0=B2, in1=B4, op=ALU.add), deps=[d])
    d = o(v, lambda e: e.tensor_scalar(out=B6, in0=B3, scalar1=16.0, scalar2=None,
                                       op0=ALU.is_ge), deps=[d])
    d = o(v, lambda e: e.tensor_tensor(out=B5, in0=B5, in1=B6, op=ALU.mult), deps=[d])
    d = o(v, lambda e: e.tensor_tensor(out=B2, in0=B2, in1=B5, op=ALU.add), deps=[d])
    d = o(v, lambda e: e.tensor_tensor(
        out=out_ap.rearrange("p (b k) -> p b k", k=16), in0=B2.rearrange("p (b k) -> p b k", k=16),
        in1=MX3.broadcast_to([128, Wb, 16]), op=ALU.mult), deps=[d])
    return d


def emit_fp4_weight(ctx, src_name, dst_name, R, C, mode, dep):
    t, d = ctx.t, ctx.dram
    trp = t["ps_s00"][:].bitcast(BF16)
    dq = dep
    for rchunk in range(R // 128):
        for s in range((C + 511) // 512):
            W = min(512, C - s * 512)
            nch = W // 128
            ldw = ctx.op("sync", lambda e, rchunk=rchunk, s=s, W=W: e.dma_start(
                out=t["w4src"][:, 0:W],
                in_=d[src_name][rchunk * 128:(rchunk + 1) * 128, s * 512:s * 512 + W]),
                deps=[dq], dma=True, dsem="sm_w4src")
            dq = fp4_quant_slice(ctx, t["w4src"][:, 0:W], t["kq_bf"][:, 0:W], W, 0, ldw)
            lastt = None
            for cix in range(nch):
                lastt = ctx.op("tensor", lambda e, cix=cix: e.transpose(
                    trp[:, cix * 128:(cix + 1) * 128], t["kq_bf"][:, cix * 128:(cix + 1) * 128],
                    t["idn_bf"][:]),
                    deps=[dq, t.get("trp_free"), t["idn_ld"]] if cix == 0 else [])
            ev = ctx.op("vector", lambda e, nch=nch: e.tensor_copy(
                t["q4stg"][:, 0:nch, :], trp.rearrange("p (a b) -> p a b", a=8)[:, 0:nch, :]),
                deps=[lastt])
            t["trp_free"] = ev
            grp = s * 4
            if mode == "tile":
                st = ctx.op("gpsimd", lambda e, grp=grp, nch=nch, rchunk=rchunk: e.dma_start(
                    out=d[dst_name][grp:grp + nch, rchunk].rearrange("a p c -> p a c"),
                    in_=t["q4stg"][:, 0:nch, :]), deps=[ev], dma=True, dsem="sm_q4st")
            else:
                st = ctx.op("gpsimd", lambda e, grp=grp, nch=nch, rchunk=rchunk: e.dma_start(
                    out=d[dst_name][grp:grp + nch, :, rchunk * 128:(rchunk + 1) * 128].rearrange("a p c -> p a c"),
                    in_=t["q4stg"][:, 0:nch, :]), deps=[ev], dma=True, dsem="sm_q4st")
            dq = st
    return dq


def emit_l1_xq(ctx, fm_deps):
    cfg, t, d = ctx.cfg, ctx.t, ctx.dram
    trp = t["ps_s00"][:].bitcast(BF16)
    dq = fm_deps[-1]
    for tt in range(cfg.TT):
        lh = ctx.op("sync", lambda e, tt=tt: e.dma_start(
            out=t["w4src"][:, 0:512], in_=d["xn32_scr"][tt * 128:(tt + 1) * 128, 0:512]),
            deps=[t["_s32d"], dq], dma=True, dsem="sm_xq32a")
        dq = fp4_quant_slice(ctx, t["w4src"][:, 0:512], t["kq_bf"][:, 0:512], 512, 0, lh)
        lh2 = ctx.op("sync", lambda e, tt=tt: e.dma_start(
            out=t["w4src"][:, 0:512], in_=d["xn32_scr"][tt * 128:(tt + 1) * 128, 512:1024]),
            deps=[dq], dma=True, dsem="sm_xq32b")
        dq = fp4_quant_slice(ctx, t["w4src"][:, 0:512], t["kq_bf"][:, 512:1024], 512, 0, lh2)
        lastt = None
        for f in range(8):
            lastt = ctx.op("tensor", lambda e, f=f: e.transpose(
                trp[:, f * 128:(f + 1) * 128], t["kq_bf"][:, f * 128:(f + 1) * 128],
                t["idn_bf"][:]), deps=[dq, t.get("trp_free")] if f == 0 else [])
        ev2 = ctx.op("vector", lambda e: e.tensor_copy(
            t["q4stg"][:, :, :], trp.rearrange("p (a b) -> p a b", a=8)), deps=[lastt])
        t["trp_free"] = ev2
        st = ctx.op("gpsimd", lambda e, tt=tt: e.dma_start(
            out=d["xq_scr"][:, tt * 128:(tt + 1) * 128].rearrange("(a p) c -> p a c", p=128),
            in_=t["q4stg"][:, :, :]), deps=[ev2], dma=True, dsem="sm_xqst")
        dq = st
    return dq


def emit_l1_gn_pass2(ctx, x_dst, attn_done, seed_all):
    cfg, t, d = ctx.cfg, ctx.t, ctx.dram
    S, T, NQ = cfg.S, cfg.T, cfg.NQ
    o = ctx.op
    lhs = dma(ctx, "sync", t["hsl_sb"][:], d["halfsel"], dsem="sm_hsl")
    lhsT = dma(ctx, "sync", t["hslT_sb"][:], d["halfselT"], dsem="sm_hslT")
    lgw = dma(ctx, "sync", t["gnwb_sb"][:, 0:2], d["l1_gnw"], dsem="sm_gnw")
    lgb = dma(ctx, "sync", t["gnwb_sb"][:, 2:4], d["l1_gnb"], dsem="sm_gnb")
    pa, pb = t["ps_a"], t["ps_b"]
    mm = o("tensor", lambda e: e.matmul(pa[0:2, 0:4], t["hsl_sb"][:], t["gn_acc"][:],
                                        start=True, stop=True),
           deps=[attn_done, lhs, t.get("psab_free"), t.get("psab_free2")])
    st = t["stats"]
    c = o("vector", lambda e: e.tensor_copy(st[0:2, 0:4], pa[0:2, 0:4]), deps=[mm])
    c = o("vector", lambda e: e.tensor_scalar(out=st[0:2, 0:4], in0=st[0:2, 0:4],
                                              scalar1=1.0 / (ctx.cfg.S * 64.0), scalar2=None,
                                              op0=ALU.mult), deps=[c])
    c = o("vector", lambda e: e.tensor_tensor(out=st[0:2, 4:6], in0=st[0:2, 0:2],
                                              in1=st[0:2, 0:2], op=ALU.mult), deps=[c])
    c = o("vector", lambda e: e.tensor_tensor(out=st[0:2, 4:6], in0=st[0:2, 2:4],
                                              in1=st[0:2, 4:6], op=ALU.subtract), deps=[c])
    c = o("vector", lambda e: e.tensor_scalar(out=st[0:2, 4:6], in0=st[0:2, 4:6],
                                              scalar1=1e-5, scalar2=None, op0=ALU.add), deps=[c])
    c = o("vector", lambda e: e.reciprocal(st[0:2, 6:8], st[0:2, 4:6]), deps=[c])
    c = o("scalar", lambda e: e.activation(st[0:2, 6:8], st[0:2, 6:8], AF.Sqrt), deps=[c])
    mm2 = o("tensor", lambda e: e.matmul(pb[:, 0:2], t["hslT_sb"][:], st[0:2, 0:2],
                                         start=True, stop=True),
            deps=[c, lhsT, t.get("psab_free2")])
    mm3 = o("tensor", lambda e: e.matmul(pb[:, 2:4], t["hslT_sb"][:], st[0:2, 6:8],
                                         start=True, stop=True), deps=[c])
    cb = o("vector", lambda e: e.tensor_copy(t["gn_tmp"][:, 0:2], pb[:, 0:2]), deps=[mm2])
    cb2 = o("vector", lambda e: e.tensor_copy(t["gn_sc"][:, 2:4], pb[:, 2:4]), deps=[mm3])
    t["psab_free2"] = cb2
    c = o("vector", lambda e: e.tensor_tensor(out=t["gn_sc"][:, 0:2], in0=t["gn_sc"][:, 2:4],
                                              in1=t["gnwb_sb"][:, 0:2], op=ALU.mult),
          deps=[cb2, lgw])
    c = o("vector", lambda e: e.tensor_tensor(out=t["gn_tmp"][:, 0:2], in0=t["gn_tmp"][:, 0:2],
                                              in1=t["gn_sc"][:, 0:2], op=ALU.mult), deps=[c, cb])
    gnf = o("vector", lambda e: e.tensor_tensor(out=t["gn_sc"][:, 2:4], in0=t["gnwb_sb"][:, 2:4],
                                                in1=t["gn_tmp"][:, 0:2], op=ALU.subtract),
            deps=[c, lgb])
    lwo = dma(ctx, "gpsimd", t["wo_sl"][:], d["l1_woT"].rearrange("(a p) m -> p a m", p=128),
              deps=[t.get("wo_sl_free")], dsem="sm_wosl")
    zro = o("vector", lambda e: e.memset(t["borow_bf"][:], 0.0), deps=[])
    ydma = {}
    yld = {}
    ysc_tot = ctx.now("sm_ysc00", "sm_ysc01", "sm_ysc10", "sm_ysc11")
    for qt in range(NQ):
        deps_o2 = []
        for hp in (0, 1):
            lt = ctx.op("sync", lambda e, qt=qt, hp=hp: e.dma_start(
                out=t["rs_bf"][:, 0, hp * 512:(hp + 1) * 512],
                in_=d["y_scr"][hp * 128:(hp + 1) * 128, qt * 512:(qt + 1) * 512]),
                deps=[attn_done, yld.get((qt - 1, hp))], dma=True, dsem=f"sm_yld{hp}")
            ts = o("vector", lambda e, hp=hp: e.tensor_scalar(
                out=t["o2"][:, hp, :], in0=t["rs_bf"][:, 0, hp * 512:(hp + 1) * 512],
                scalar1=t["gn_sc"][:, 0 + hp:1 + hp], scalar2=t["gn_sc"][:, 2 + hp:3 + hp],
                op0=ALU.mult, op1=ALU.add),
                deps=[lt, gnf, t.get(("l1o2f", hp))] + (ysc_tot if qt == 0 else []))
            yld[(qt, hp)] = ts
            deps_o2.append(ts)
        if qt == 0:
            deps_o2 += [t.get("psab_free"), t.get("psab_free2")]
        lw = wo_window(ctx, qt, deps_o2, lwo, zro, ydma, [t["ps_a"], t["ps_b"]])
        t["psab_free"] = lw
        t["psab_free2"] = lw
        t[("l1o2f", 0)] = lw
        t[("l1o2f", 1)] = lw
    t["wo_sl_free"] = ctx.now("pe")[0]
    return [ydma[(qt, nb, 3)] for qt in range(NQ) for nb in (0, 1)]


def emit_l1_ffn(ctx, x_dst, xq_done, attn_done, seed_all):
    cfg, t, d = ctx.cfg, ctx.t, ctx.dram
    T, TT = cfg.T, cfg.TT
    pa, pb = t["ps_a"], t["ps_b"]
    o = ctx.op
    MW = [(i * 512, min(512, 2816 - i * 512)) for i in range((2816 + 511) // 512)]
    hdma = {}
    xqld = {}
    for mwi, (m0, mw) in enumerate(MW):
        nmt = mw // 128
        lg = lu = None
        for kc in range(8):
            lg = ctx.op("gpsimd", lambda e, m0=m0, nmt=nmt, kc=kc: e.dma_start(
                out=t["w_rhs"][:, kc, 0:nmt * 128].rearrange("p (b c) -> p b c", c=128),
                in_=d["wgq"][kc, m0 // 128:m0 // 128 + nmt, :, :].rearrange("b p c -> p b c")),
                deps=[t.get("wgq_done"), t.get("w_rhs_free"), t.get(("wrf", mwi - 1)),
                      t.get(("wrf2", mwi - 1))] if kc == 0 else [],
                dma=True, dsem="sm_l1wg")
            lu = ctx.op("gpsimd", lambda e, m0=m0, nmt=nmt, kc=kc: e.dma_start(
                out=t["w_rhs"][:, kc, 512:512 + nmt * 128].rearrange("p (b c) -> p b c", c=128),
                in_=d["wuq"][kc, m0 // 128:m0 // 128 + nmt, :, :].rearrange("b p c -> p b c")),
                deps=[t.get("wuq_done")] if kc == 0 else [], dma=True, dsem="sm_l1wu")
        for tt in range(TT):
            lx = ctx.op("sync", lambda e, tt=tt: e.dma_start(
                out=t["xn_strm"][:, 0, :, 0:128],
                in_=d["xq_scr"][:, tt * 128:(tt + 1) * 128].rearrange("(a p) c -> p a c", p=128)),
                deps=[xq_done, xqld.get((mwi, tt - 1)), xqld.get((mwi - 1, TT - 1)) if tt == 0 else None], dma=True, dsem="sm_xnstrm0")
            mmg = None
            mmu = None
            for kc in range(8):
                mmg = o("tensor", lambda e, kc=kc, mw=mw: e.matmul(
                    pa[:, 0:mw], t["xn_strm"][:, 0, kc, 0:128], t["w_rhs"][:, kc, 0:mw],
                    start=(kc == 0), stop=(kc == 7)),
                    deps=[lx, lg, t.get("psab_free")] if kc == 0 else [])
            for kc in range(8):
                mmu = o("tensor", lambda e, kc=kc, mw=mw: e.matmul(
                    pb[:, 0:mw], t["xn_strm"][:, 0, kc, 0:128], t["w_rhs"][:, kc, 512:512 + mw],
                    start=(kc == 0), stop=(kc == 7)),
                    deps=[lu, t.get("psab_free2")] if kc == 0 else [])
            sil = o("scalar", lambda e, mw=mw: e.activation(
                t["q4b"][:, 0:mw], pa[:, 0:mw], AF.Silu),
                deps=[mmg, t.get("l1f_hd_prev")])
            t["psab_free"] = sil
            upe = o("vector", lambda e, mw=mw: e.tensor_copy(
                t["q4b2"][:, 0:mw], pb[:, 0:mw]),
                deps=[mmu, t.get("l1f_hm_prev")])
            t["psab_free2"] = upe
            hmul = o("vector", lambda e, mw=mw: e.tensor_tensor(
                out=t["q4b"][:, 0:mw], in0=t["q4b"][:, 0:mw],
                in1=t["q4b2"][:, 0:mw], op=ALU.mult), deps=[sil, upe])
            t["l1f_hm_prev"] = hmul
            hd = ctx.op("gpsimd", lambda e, tt=tt, m0=m0, mw=mw: e.dma_start(
                out=d["h_scr"][tt * 128:(tt + 1) * 128, m0:m0 + mw],
                in_=t["q4b"][:, 0:mw]), deps=[hmul], dma=True, dsem="sm_hst")
            hdma[(mwi, tt)] = hd
            t["l1f_hd_prev"] = hd
            xqld[(mwi, tt)] = mmu
            t[("wrf", mwi)] = mmg
            t[("wrf2", mwi)] = mmu
    gu_done = ctx.now("sm_hst")[0]
    dq = gu_done
    hq_st = {}
    trp = t["ps_s01"][:].bitcast(BF16)
    for tt in range(TT):
        for s in range((2816 + 511) // 512):
            W = min(512, 2816 - s * 512)
            nch = W // 128
            lh = ctx.op("sync", lambda e, tt=tt, s=s, W=W: e.dma_start(
                out=t["w4src"][:, 0:W], in_=d["h_scr"][tt * 128:(tt + 1) * 128, s * 512:s * 512 + W]),
                deps=[hdma[(s, tt)] if (s, tt) in hdma else hdma[(len(MW) - 1, tt)], dq], dma=True, dsem="sm_hld")
            dq = fp4_quant_slice(ctx, t["w4src"][:, 0:W], t["h_sl"][:, 0:W], W, 0, lh)
            lastt = None
            for cix in range(nch):
                lastt = ctx.op("tensor", lambda e, cix=cix: e.transpose(
                    trp[:, cix * 128:(cix + 1) * 128], t["h_sl"][:, cix * 128:(cix + 1) * 128],
                    t["idn_bf"][:]),
                    deps=[dq, t.get("trp1_free"), attn_done] if cix == 0 else [])
            ev = ctx.op("vector", lambda e, nch=nch: e.tensor_copy(
                t["q4stg"][:, 0:nch, :], trp.rearrange("p (a b) -> p a b", a=8)[:, 0:nch, :]),
                deps=[lastt])
            t["trp1_free"] = ev
            grp = s * 4
            st = ctx.op("gpsimd", lambda e, grp=grp, nch=nch, tt=tt: e.dma_start(
                out=d["hq_scr"][grp:grp + nch, tt].rearrange("a p c -> p a c"),
                in_=t["q4stg"][:, 0:nch, :]), deps=[ev], dma=True, dsem="sm_hqst")
            dq = st
        hq_st[tt] = dq
    banks = [t["ps_s00"], t["ps_s01"], t["ps_s10"], t["ps_s11"],
             t["ps_o0"], t["ps_o1"], pa, pb]
    acc_t = {}
    rs_tot = ctx.now("sm_rsacc0", "sm_rsacc1", "sm_ffnacc0", "sm_ffnacc1")
    for tw in range(T // 512):
        mml = {}
        for ic in range(22):
            sl = ic % 2
            lw = ctx.op("gpsimd", lambda e, ic=ic, sl=sl: e.dma_start(
                out=t["w_rhs"][:, sl * 4, :], in_=d["wdq"][ic, :, :]),
                deps=[t.get("wdq_done"), t.get(("dwf", tw, ic - 2)), t.get(("wrf", len(MW) - 1)),
                      hq_st[TT - 1]],
                dma=True, dsem=f"sm_dw{sl}")
            lx = ctx.op("sync", lambda e, ic=ic, tw=tw, sl=sl: e.dma_start(
                out=t["xn_strm"][:, 0, sl * 4:sl * 4 + 4, 0:128],
                in_=d["hq_scr"][ic, tw * 4:(tw + 1) * 4, :, :].rearrange("a p c -> p a c")),
                deps=[hq_st[TT - 1], t.get(("dxf", tw, ic - 2))],
                dma=True, dsem=f"sm_dx{sl}")
            for bi in range(8):
                ttl, nb = bi // 2, bi % 2
                mm = o("tensor", lambda e, ic=ic, sl=sl, ttl=ttl, nb=nb, bi=bi: e.matmul(
                    banks[bi][:], t["xn_strm"][:, 0, sl * 4 + ttl, 0:128],
                    t["w_rhs"][:, sl * 4, nb * 512:(nb + 1) * 512],
                    start=(ic == 0), stop=(ic == 21)),
                    deps=([lw, lx, attn_done, t.get(("dbf", tw - 1, bi)), t.get("trp1_free"),
                           t.get("psab_free"), t.get("psab_free2"), t.get("wo_sl_free")]
                          if ic == 0 else ([lw, lx] if bi == 0 else [])))
                mml[bi] = mm
            t[("dwf", tw, ic)] = mml[7]
            t[("dxf", tw, ic)] = mml[7]
        for bi in range(8):
            ttl, nb = bi // 2, bi % 2
            tt = tw * 4 + ttl
            ev = o("vector", lambda e, bi=bi, nb=nb: e.tensor_copy(
                t["evac"][:, nb, :], banks[bi][:]),
                deps=[mml[bi], acc_t.get((tw, bi - 2))])
            acc = dma_acc(ctx, x_dst[tt * 128:(tt + 1) * 128, nb * 512:(nb + 1) * 512],
                          t["evac"][:, nb, :], deps=[ev, seed_all] + rs_tot,
                          dsem=f"sm_l1acc{nb}")
            acc_t[(tw, bi)] = acc
            t[("dbf", tw, bi)] = ev
    return acc_t


def emit_layer1(ctx, x_src, x_dst):
    cfg, t, d = ctx.cfg, ctx.t, ctx.dram
    TT = cfg.TT
    prior = ctx.now("d_pool", "d_scal", "sm_seed", "sm_ffnacc0", "sm_ffnacc1",
                    "sm_rsacc0", "sm_rsacc1", "sm_l1acc0", "sm_l1acc1",
                    "sm_l2acc0", "sm_l2acc1") + [t.get("l0_done"), t.get("layer_done")]
    seed_t = [dma(ctx, "sync", x_dst[i * 128:(i + 1) * 128, :],
                  x_src[i * 128:(i + 1) * 128, :], deps=prior, dsem="sm_seed")
              for i in range(TT)]
    seed_all = seed_t[-1]
    t["seed_all"] = seed_all
    nt = emit_norm(ctx, x_src, kind="ln", deps=prior, store32="xn32_scr")
    ag_done = emit_allgather_xn(ctx, "ag0", deps=nt["fm"])
    wg_done = emit_fp4_weight(ctx, "l1_wg", "wgq", 2816, 1024, "tile",
                              t.get("_s32d") or nt["tm"][-1])
    wu_done = emit_fp4_weight(ctx, "l1_wu", "wuq", 2816, 1024, "tile", wg_done)
    wd_done = emit_fp4_weight(ctx, "l1_wd", "wdq", 1024, 2816, "rows", wu_done)
    t["wgq_done"] = wg_done
    t["wuq_done"] = wu_done
    t["wdq_done"] = wd_done
    xq_done = emit_l1_xq(ctx, nt["fm"] + [wd_done])
    lqd = dma(ctx, "gpsimd", t["qd_bf"][:], d["l1_qd"], deps=[t.get("l0_done")],
              dsem="sm_qd")
    qst = [t["qd_bf"][:, 0, :], t["qd_bf"][:, 1, :]]
    t["qkv_done"] = emit_qkv(ctx, "l1_wqT", "l1_wkT", "l1_wvT", None, None, None,
                             "ag0", ag_done, q_scale_tiles=qst, q_scale_dep=lqd)
    attn_done = emit_attention(ctx, "l1")
    ydeps = emit_l1_gn_pass2(ctx, x_dst, attn_done, seed_all)
    emit_rs_residual(ctx, x_dst, ydeps, seed_t)
    emit_l1_ffn(ctx, x_dst, xq_done, attn_done, seed_all)
    t["l0_done"] = ctx.now("d_pool")[0]



def emit_l2_butterfly(ctx, subj_src, width, a_name, b_name, out_bf_cb, dep):
    cfg, t, d = ctx.cfg, ctx.t, ctx.dram
    trans = a_name == "l2_dn_a"
    cur = subj_src
    ping = [d["bfa"], d["bfb"]]
    AB = t["xq_mx"]
    PM = t["w4src"][:, 0:128]
    dq = dep
    for ent in range(12):
        stg = (11 - ent) if trans else ent
        s = 2 ** stg
        la = ctx.op("gpsimd", lambda e, ent=ent: e.dma_start(
            out=AB[:, 0, 0:32], in_=d[a_name][ent]), deps=[dq], dma=True, dsem="sm_l2a")
        lb = ctx.op("gpsimd", lambda e, ent=ent: e.dma_start(
            out=AB[:, 1, 0:32], in_=d[b_name][ent]), deps=[dq], dma=True, dsem="sm_l2b")
        if s < 128:
            lp = ctx.op("gpsimd", lambda e, stg=stg: e.dma_start(
                out=PM, in_=d["l2_perm"][stg]), deps=[dq], dma=True, dsem="sm_l2p")
        dst = ping[ent % 2]
        for blk in range(8):
            c0 = blk * 4
            ld1 = ctx.op("sync", lambda e, cur=cur, c0=c0, width=width: e.dma_start(
                out=t["q4b"][:, 0:4 * width].rearrange("p (a w) -> p a w", a=4),
                in_=cur[c0 * 128:(c0 + 4) * 128, 0:width].rearrange("(a p) w -> p a w", p=128)),
                deps=[dq, la, lb, t.get("bf_prev_ad")] + ([lp] if s < 128 else []),
                dma=True, dsem="sm_bf1")
            ads = []
            for cx in range(4):
                c = c0 + cx
                if s < 128:
                    mm = ctx.op("tensor", lambda e, cx=cx, width=width: e.matmul(
                        t["ps_a"][:, 0:width], PM, t["q4b"][:, cx * width:(cx + 1) * width],
                        start=True, stop=True),
                        deps=[ld1, t.get("l2pm_free"), t.get("psab_free")])
                    t2 = ctx.op("vector", lambda e, c=c, cx=cx, width=width: e.tensor_scalar(
                        out=t["q4b2"][:, cx * width:(cx + 1) * width], in0=t["ps_a"][:, 0:width],
                        scalar1=AB[:, 1, c % 32:c % 32 + 1], scalar2=None, op0=ALU.mult),
                        deps=[mm])
                    t["l2pm_free"] = t2
                    t["psab_free"] = t2
                else:
                    pc = c ^ (s // 128)
                    ldp = ctx.op("sync", lambda e, cur=cur, pc=pc, cx=cx, width=width: e.dma_start(
                        out=t["q4b2"][:, cx * width:(cx + 1) * width],
                        in_=cur[pc * 128:(pc + 1) * 128, 0:width]),
                        deps=[dq], dma=True, dsem=f"sm_bf2{cx}")
                    t2 = ctx.op("vector", lambda e, c=c, cx=cx, width=width: e.tensor_scalar(
                        out=t["q4b2"][:, cx * width:(cx + 1) * width],
                        in0=t["q4b2"][:, cx * width:(cx + 1) * width],
                        scalar1=AB[:, 1, c % 32:c % 32 + 1], scalar2=None, op0=ALU.mult),
                        deps=[ldp, la, lb])
                t1 = ctx.op("vector", lambda e, c=c, cx=cx, width=width: e.tensor_scalar(
                    out=t["q4b"][:, cx * width:(cx + 1) * width],
                    in0=t["q4b"][:, cx * width:(cx + 1) * width],
                    scalar1=AB[:, 0, c % 32:c % 32 + 1], scalar2=None, op0=ALU.mult),
                    deps=[t2, ld1, la, lb])
                ads.append(ctx.op("vector", lambda e, cx=cx, width=width: e.tensor_tensor(
                    out=t["q4b"][:, cx * width:(cx + 1) * width],
                    in0=t["q4b"][:, cx * width:(cx + 1) * width],
                    in1=t["q4b2"][:, cx * width:(cx + 1) * width], op=ALU.add), deps=[t1]))
            st = ctx.op("gpsimd", lambda e, dst=dst, c0=c0, width=width: e.dma_start(
                out=dst[c0 * 128:(c0 + 4) * 128, 0:width].rearrange("(a p) w -> p a w", p=128),
                in_=t["q4b"][:, 0:4 * width].rearrange("p (a w) -> p a w", a=4)),
                deps=ads, dma=True, dsem="sm_bf3")
            t["bf_prev_ad"] = ads[-1]
            dq = st
        cur = dst
    return out_bf_cb(cur, dq)


def emit_layer2(ctx, x_src, x_dst):
    cfg, t, d = ctx.cfg, ctx.t, ctx.dram
    S, T, TT = cfg.S, cfg.T, cfg.TT
    ACCS = ("d_pool", "d_scal", "sm_seed", "sm_ffnacc0", "sm_ffnacc1", "sm_rsacc0",
            "sm_rsacc1", "sm_l1acc0", "sm_l1acc1", "sm_l2acc0", "sm_l2acc1")
    prior = ctx.now(*ACCS) + [t.get("layer_done")]
    seed_t = [dma(ctx, "sync", x_dst[i * 128:(i + 1) * 128, :],
                  x_src[i * 128:(i + 1) * 128, :], deps=prior, dsem="sm_seed")
              for i in range(TT)]
    seed_all = seed_t[-1]
    t["seed_all"] = seed_all
    nt = emit_norm(ctx, x_src, kind="ln", deps=prior)
    ag_done = emit_allgather_xn(ctx, "ag0", deps=nt["fm"])
    t["qkv_done"] = emit_qkv(ctx, "l2_wq", "l2_wk", "l2_wv",
                             "l2_bq", "l2_bk", "l2_bv", "ag0", ag_done)
    bdep = nt["tm"][-1]

    def up_out(cur, dq):
        trp = t["ps_s00"][:].bitcast(BF16)
        last = dq
        for blk in range(8):
            ld1 = ctx.op("sync", lambda e, cur=cur, blk=blk: e.dma_start(
                out=t["q4b"][:, 0:512].rearrange("p (a w) -> p a w", a=4),
                in_=cur[blk * 512:(blk + 1) * 512, :].rearrange("(a p) w -> p a w", p=128)),
                deps=[last], dma=True, dsem="sm_bf1")
            cv = ctx.op("vector", lambda e: e.tensor_copy(
                t["kq_bf"][:, 0:512], t["q4b"][:, 0:512]), deps=[ld1, t.get("trp_free2")])
            lastt = None
            for cx in range(4):
                lastt = ctx.op("tensor", lambda e, cx=cx: e.transpose(
                    trp[:, cx * 128:(cx + 1) * 128], t["kq_bf"][:, cx * 128:(cx + 1) * 128],
                    t["idn_bf"][:]), deps=[cv, t.get("trp_free"), t["idn_ld"]] if cx == 0 else [])
            ev = ctx.op("vector", lambda e: e.tensor_copy(
                t["q4stg"][:, 0:4, :], trp.rearrange("p (a b) -> p a b", a=8)[:, 0:4, :]),
                deps=[lastt])
            t["trp_free"] = ev
            t["trp_free2"] = ev
            st = ctx.op("gpsimd", lambda e, blk=blk: e.dma_start(
                out=d["agu_in"][:, blk * 512:(blk + 1) * 512].rearrange("p (a c) -> p a c", a=4),
                in_=t["q4stg"][:, 0:4, :]), deps=[ev], dma=True, dsem="sm_agu")
            last = st
        return ctx.op("gpsimd", lambda e: e.collective_compute(
            "AllGather", ALU.bypass, replica_groups=GROUPS8,
            ins=[d["agu_in"].opt()], outs=[d["agu_out"].opt()]), deps=[last], cc=True,
            dsem="cc_agu")

    def dn_out(cur, dq):
        last = dq
        for blk in range(8):
            ld1 = ctx.op("sync", lambda e, cur=cur, blk=blk: e.dma_start(
                out=t["q4b"][:, 0:512].rearrange("p (a w) -> p a w", a=4),
                in_=cur[blk * 512:(blk + 1) * 512, :].rearrange("(a p) w -> p a w", p=128)),
                deps=[last], dma=True, dsem="sm_bf1")
            cv = ctx.op("vector", lambda e: e.tensor_copy(
                t["q4stg"][:, 0:4, :], t["q4b"][:, 0:512].rearrange("p (a w) -> p a w", a=4)),
                deps=[ld1, t.get("trp_free2")])
            t["trp_free2"] = cv
            st = ctx.op("gpsimd", lambda e, blk=blk: e.dma_start(
                out=d["agd_in"][blk * 512:(blk + 1) * 512, :].rearrange("(a p) w -> p a w", p=128),
                in_=t["q4stg"][:, 0:4, :]), deps=[cv], dma=True, dsem="sm_agd")
            last = st
        return ctx.op("gpsimd", lambda e: e.collective_compute(
            "AllGather", ALU.bypass, replica_groups=GROUPS8,
            ins=[d["agd_in"].opt()], outs=[d["agd_out"].opt()]), deps=[last], cc=True,
            dsem="cc_agd")

    def bx_out(cur, dq):
        ld1 = ctx.op("gpsimd", lambda e, cur=cur: e.dma_start(
            out=t["gelu_b"][:, 0:32], in_=cur[:, 0:1].rearrange("(c p) o -> p (c o)", p=128),
            allow_slow_non_contiguous=True),
            deps=[dq], dma=True, dsem="sm_bx")
        lup = ctx.op("gpsimd", lambda e: e.dma_start(
            out=t["gelu_b"][:, 32:64], in_=d["l2_upb"]), dma=True, dsem="sm_upb")
        return ctx.op("vector", lambda e: e.tensor_tensor(
            out=t["gelu_b"][:, 0:32], in0=t["gelu_b"][:, 0:32], in1=t["gelu_b"][:, 32:64],
            op=ALU.add), deps=[ld1, lup])

    up_cc = emit_l2_butterfly(ctx, d["l2_wexpT_sl"], 128, "l2_up_a", "l2_up_b", up_out, bdep)
    bx_ad = emit_l2_butterfly(ctx, d["l2_bexp"], 4, "l2_up_a", "l2_up_b", bx_out, up_cc)
    dn_cc = emit_l2_butterfly(ctx, d["l2_wcon_sl"], 128, "l2_dn_a", "l2_dn_b", dn_out, bx_ad)

    ydeps = emit_attention(ctx, "l2", wo_name="l2_wproj", bo_name="l2_bproj")
    emit_rs_residual(ctx, x_dst, ydeps, seed_t)

    pa, pb = t["ps_a"], t["ps_b"]
    o = ctx.op
    attn_end = ctx.now("act")[0]
    lbc = dma(ctx, "gpsimd", t["brow_bf2"][:], d["l2_bcon"], dsem="sm_fbias")
    lbd = dma(ctx, "gpsimd", t["borow_bf"][:], d["l2_dnb"],
              deps=ctx.now("sm_ybf0", "sm_ybf1"), dsem="sm_borow")
    wld = {}
    hst = {}
    for ic in range(32):
        sl = ic % 2
        lw = ctx.op("gpsimd", lambda e, ic=ic, sl=sl: e.dma_start(
            out=t["w_rhs"][:, 0:8, sl * 128:(sl + 1) * 128],
            in_=d["agu_out"][:, ic * 128:(ic + 1) * 128].rearrange("(a p) m -> p a m", p=128)),
            deps=[up_cc, wld.get(ic - 2), t.get("w_rhs_free")], dma=True, dsem=f"sm_uw{sl}")
        for tw in range(T // 512):
            ps = pa if tw % 2 == 0 else pb
            pk = "psab_free" if tw % 2 == 0 else "psab_free2"
            mm = None
            for kc in range(8):
                mm = o("tensor", lambda e, sl=sl, kc=kc, tw=tw, ps=ps: e.matmul(
                    ps[:], t["w_rhs"][:, kc, sl * 128:(sl + 1) * 128],
                    t["xn_fm"][:, kc, tw * 512:(tw + 1) * 512], start=(kc == 0), stop=(kc == 7)),
                    deps=[lw, t.get(pk),
                          attn_end if (ic == 0 and tw == 0) else None,
                          nt["fm"][-1]] if kc == 0 else [])
            gl = o("scalar", lambda e, ic=ic, ps=ps: e.activation(
                t["l1f_g"][:, 0, :], ps[:], AF.Gelu, bias=t["gelu_b"][:, ic:ic + 1], scale=1.0),
                deps=[mm, bx_ad, t.get("l2h_free")])
            t[pk] = gl
            hd = ctx.op("gpsimd", lambda e, ic=ic, tw=tw: e.dma_start(
                out=d["h2_scr"][ic, :, tw * 512:(tw + 1) * 512], in_=t["l1f_g"][:, 0, :]),
                deps=[gl], dma=True, dsem="sm_h2")
            t["l2h_free"] = hd
            hst[(ic, tw)] = hd
        wld[ic] = mm
    t["w_rhs_free"] = wld[31]
    hs_tot = ctx.now("sm_h2")
    banks = [t["ps_s00"], t["ps_s01"], t["ps_s10"], t["ps_s11"],
             t["ps_o0"], t["ps_o1"], pa, pb]
    rs_tot = ctx.now("sm_rsacc0", "sm_rsacc1")
    l2acc = {}
    for tw in range(T // 512):
        mml = {}
        for ic in range(32):
            sl = ic % 2
            lw = ctx.op("gpsimd", lambda e, ic=ic, sl=sl: e.dma_start(
                out=t["w_rhs"][:, sl * 4, :].rearrange("p (a c) -> p a c", a=8),
                in_=d["agd_out"].rearrange("(r q) w -> r q w", r=8)[:, ic * 128:(ic + 1) * 128, :].rearrange("r p w -> p r w")),
                deps=[dn_cc, t.get(("l2dw", tw, ic - 2)), wld[31] if (tw == 0 and ic == 0) else None],
                dma=True, dsem=f"sm_dw{sl}")
            lx = ctx.op("sync", lambda e, ic=ic, tw=tw, sl=sl: e.dma_start(
                out=t["xn_strm"][:, 0, sl * 4:sl * 4 + 4, 0:128],
                in_=d["h2_scr"][ic, :, tw * 512:(tw + 1) * 512].rearrange("p (a c) -> p a c", c=128)),
                deps=hs_tot + [t.get(("l2dxf", tw, ic - 2))], dma=True, dsem=f"sm_dx{sl}")
            for bi in range(8):
                ttl, nb = bi // 2, bi % 2
                mm = o("tensor", lambda e, ic=ic, sl=sl, ttl=ttl, nb=nb, bi=bi: e.matmul(
                    banks[bi][:, 0:512], t["xn_strm"][:, 0, sl * 4 + ttl, 0:128],
                    t["w_rhs"][:, sl * 4, nb * 512:(nb + 1) * 512],
                    start=(ic == 0), stop=False),
                    deps=([lw, lx, attn_end, t.get(("l2db", tw - 1, bi)),
                           t.get("psab_free"), t.get("psab_free2"), t.get("wo_sl_free"),
                           t.get("trp_free"), t.get("trp_free2")]
                          if ic == 0 else ([lw, lx] if bi == 0 else [])))
                mml[bi] = mm
            t[("l2dw", tw, ic)] = mml[7]
            t[("l2dxf", tw, ic)] = mml[7]
        for bi in range(8):
            ttl, nb = bi // 2, bi % 2
            tt = tw * 4 + ttl
            m2 = o("tensor", lambda e, bi=bi, nb=nb: e.matmul(
                banks[bi][:, 0:512], t["ones1b"][:], t["brow_bf2"][:, nb * 512:(nb + 1) * 512],
                start=False, stop=False), deps=[lbc, mml[bi]])
            m2 = o("tensor", lambda e, bi=bi, nb=nb: e.matmul(
                banks[bi][:, 0:512], t["ones1b"][:], t["borow_bf"][:, nb * 512:(nb + 1) * 512],
                start=False, stop=True), deps=[lbd])
            ev = o("vector", lambda e, bi=bi, nb=nb: e.tensor_copy(
                t["evac"][:, nb, :], banks[bi][:, 0:512]),
                deps=[m2, l2acc.get((tw, bi - 2))])
            acc = dma_acc(ctx, x_dst[tt * 128:(tt + 1) * 128, nb * 512:(nb + 1) * 512],
                          t["evac"][:, nb, :], deps=[ev, seed_all] + rs_tot,
                          dsem=f"sm_l2acc{nb}")
            l2acc[(tw, bi)] = acc
            t[("l2db", tw, bi)] = ev
    t["layer_done"] = ctx.now("pe")[0]





def emit_l3_moe(ctx, x_dst, nt, seed_all):
    cfg, t, d = ctx.cfg, ctx.t, ctx.dram
    TT = cfg.TT
    o = ctx.op
    v = "vector"
    lo_fm = t["kt_fm"][:].rearrange("p a s -> p (a s)").rearrange(
        "p (c tt) -> p c tt", c=8)
    glo32 = t["scr_f32"][:, 0:64].rearrange("p (a b) -> p a b", a=8)
    lgw = dma(ctx, "sync", t["gw_sb"][:], d["l3_gatewT"].rearrange("(a p) e -> p a e", p=128),
              deps=[t["r0m_ld"]], dsem="sm_r0m")
    c1 = o(v, lambda e: e.tensor_copy(t["gwhi"][:], t["gw_sb"][:]),
           deps=[lgw, nt["tm"][-1], t.get("xstage_free")])
    c2 = o(v, lambda e: e.tensor_tensor(out=glo32, in0=t["gw_sb"][:], in1=t["gwhi"][:],
                                        op=ALU.subtract), deps=[c1, nt["s32"][-1]])
    c3 = o(v, lambda e: e.tensor_copy(t["gwlo"][:], glo32), deps=[c2])
    trp = t["ps_s10"][:].bitcast(BF16)
    s32 = nt["s32"]
    prevhi = {}
    oht_t = {}
    ev_prev = None
    for i in range(TT):
        sl = i % 2
        ldx = dma(ctx, "sync", t["x_stage"][:, sl, :], d["xn32b_scr"][i * 128:(i + 1) * 128, :],
                  deps=[s32[i], prevhi.get(i - 2),
                        t.get("xstage_free") if i < 2 else None], dsem=f"sm_xst{sl}")
        hb = o(v, lambda e, sl=sl: e.tensor_copy(t["kq_bf"][:], t["x_stage"][:, sl, :]),
               deps=[ldx, c3])
        lo = o(v, lambda e, sl=sl: e.tensor_tensor(
            out=t["scr_f32"][:], in0=t["x_stage"][:, sl, :], in1=t["kq_bf"][:],
            op=ALU.subtract), deps=[hb])
        lb = o(v, lambda e, sl=sl: e.tensor_copy(t["xn_tm"][:, sl, :], t["scr_f32"][:]),
               deps=[lo])
        tlast = None
        for f in range(8):
            tlast = o("tensor", lambda e, sl=sl, f=f: e.transpose(
                trp[:, f * 128:(f + 1) * 128], t["xn_tm"][:, sl, f * 128:(f + 1) * 128],
                t["idn_bf"][:]),
                deps=[lb, t["idn_ld"], prevhi.get(i - 1)] if f == 0 else [])
        cp = o(v, lambda e, i=i: e.tensor_copy(
            lo_fm[:, :, i * 128:(i + 1) * 128], trp.rearrange("p (a b) -> p a b", a=8)),
            deps=[tlast])
        prevhi[i] = cp
        mm = None
        passes = [(t["xn_fm"], t["gwhi"]), (lo_fm, t["gwhi"]), (t["xn_fm"], t["gwlo"])]
        for pi, (lhs, rhs) in enumerate(passes):
            for kc in range(8):
                first = (pi == 0 and kc == 0)
                mm = o("tensor", lambda e, lhs=lhs, rhs=rhs, kc=kc, pi=pi, i=i: e.matmul(
                    t["ps_s01"][:, 0:8], lhs[:, kc, i * 128:(i + 1) * 128], rhs[:, kc, :],
                    start=(pi == 0 and kc == 0), stop=(pi == 2 and kc == 7)),
                    deps=[cp, c3, nt["fm"][i], ev_prev] if first else [])
        ev = o(v, lambda e: e.tensor_copy(t["lg_sb"][:, 0:8], t["ps_s01"][:, 0:8]), deps=[mm])
        ev_prev = ev
        mx = o(v, lambda e: e.tensor_reduce(t["lg_sb"][:, 8:9], t["lg_sb"][:, 0:8],
                                            AX.X, ALU.max), deps=[ev])
        ohc = o(v, lambda e, i=i: e.tensor_scalar(
            out=t["oh_full"][:, i * 8:(i + 1) * 8], in0=t["lg_sb"][:, 0:8],
            scalar1=t["lg_sb"][:, 8:9],
            scalar2=None, op0=ALU.is_ge), deps=[mx])
        oht_t[i] = ohc
    lg_done = ctx.now("pe")[0]
    wfree = lg_done
    mprev = {}
    guprev = {}
    acc_t = {}
    dn_prev = {}
    for e_ in range(8):
        lwg = dma(ctx, "gpsimd", t["w_q"][:],
                  d["l3_wg"][e_:e_ + 1, :, :].rearrange("o (a p) m -> p (o a) m", p=128),
                  deps=[wfree], dsem="sm_wq")
        lwu = dma(ctx, "gpsimd", t["w_k"][:],
                  d["l3_wu"][e_:e_ + 1, :, :].rearrange("o (a p) m -> p (o a) m", p=128),
                  deps=[wfree], dsem="sm_wk")
        lwd = dma(ctx, "gpsimd", t["wo_sl"][:],
                  d["l3_wd"][e_:e_ + 1, :, :].rearrange("o (a p) m -> p (o a) m", p=128),
                  deps=[wfree, t.get("wo_sl_free")], dsem="sm_wosl")
        t["wo_sl_free"] = None
        for i in range(TT):
            it = e_ * TT + i
            par = it % 2
            bGU = t["ps_s00"] if par == 0 else t["ps_s10"]
            pd = [t["ps_o0"], t["ps_o1"]] if par == 0 else [t["ps_a"], t["ps_b"]]
            gm = None
            for wi, wt in ((0, t["w_q"]), (1, t["w_k"])):
                for mc in (0, 1):
                    for kc in range(8):
                        first = (wi == 0 and mc == 0 and kc == 0)
                        gm = o("tensor", lambda e, wi=wi, wt=wt, mc=mc, kc=kc, i=i, bGU=bGU: e.matmul(
                            bGU[:, wi * 256 + mc * 128:wi * 256 + (mc + 1) * 128],
                            wt[:, kc, mc * 128:(mc + 1) * 128],
                            t["xn_fm"][:, kc, i * 128:(i + 1) * 128],
                            start=(kc == 0), stop=(kc == 7)),
                            deps=[lwg, lwu, guprev.get(it - 2),
                                  prevhi[TT - 1] if it == 0 else None,
                                  nt["fm"][TT - 1] if it == 0 else None] if first else [])
            rl = o("scalar", lambda e, bGU=bGU: e.activation(
                t["scr_f32"][:, 0:256], bGU[:, 0:256], AF.Relu),
                deps=[gm, mprev.get(it - 1)])
            sq = o(v, lambda e: e.tensor_tensor(
                out=t["scr_f32"][:, 256:512], in0=t["scr_f32"][:, 0:256],
                in1=t["scr_f32"][:, 0:256], op=ALU.mult), deps=[rl])
            mu = o(v, lambda e, bGU=bGU: e.tensor_tensor(
                out=t["l1f_g"][:, 0, 0:256], in0=t["scr_f32"][:, 256:512],
                in1=bGU[:, 256:512], op=ALU.mult),
                deps=[sq, prevhi.get(TT - 1) if it == 0 else None])
            mprev[it] = mu
            guprev[it] = mu
            dm = None
            for dc in (0, 1):
                for nb in (0, 1):
                    first = (dc == 0 and nb == 0)
                    dm = o("tensor", lambda e, dc=dc, nb=nb, pd=pd: e.matmul(
                        pd[nb][:], t["l1f_g"][:, 0, dc * 128:(dc + 1) * 128],
                        t["wo_sl"][:, dc, nb * 512:(nb + 1) * 512],
                        start=(dc == 0), stop=(dc == 1)),
                        deps=[mu, lwd, dn_prev.get(it - 2)] if first else [])
            for nb in (0, 1):
                evc = o(v, lambda e, nb=nb, pd=pd, e_=e_, i=i: e.tensor_scalar(
                    out=t["evac"][:, nb, :], in0=pd[nb][:],
                    scalar1=t["oh_full"][:, i * 8 + e_:i * 8 + e_ + 1],
                    scalar2=None, op0=ALU.mult),
                    deps=[dm, acc_t.get((it - 1, nb)), oht_t[i]])
                ac = dma_acc(ctx, x_dst[i * 128:(i + 1) * 128, nb * 512:(nb + 1) * 512],
                             t["evac"][:, nb, :], deps=[evc, seed_all],
                             dsem=f"sm_l1acc{nb}")
                acc_t[(it, nb)] = ac
                dn_prev[it] = evc
        wfree = gm
    res = {"pe": ctx.now("pe")[0], "dve": ctx.now("dve")[0],
           "accs": ctx.now("sm_l1acc0", "sm_l1acc1")}
    t["psab_free"] = res["dve"]
    return res


def emit_l3_kvq(ctx, mode, ag_done, war_pe, war_dve, war_qscr=None):
    cfg, t, d = ctx.cfg, ctx.t, ctx.dram
    S, T = cfg.S, cfg.T
    NW = S // 512
    o = ctx.op
    v = "vector"
    glob = mode == "g"
    tab = d["l3_ropeg" if glob else "l3_ropel"]
    if glob:
        lw1 = dma(ctx, "gpsimd", t["w_v"][:], d["l3_gkvd"].rearrange("(a p) m -> p a m", p=128),
                  deps=[war_pe], dsem="sm_wv")
        lw2 = dma(ctx, "gpsimd", t["kvup_sb"][:], d["l3_gkvup"].rearrange("(a p) m -> p a m", p=128),
                  deps=[war_pe], dsem="sm_bq")
    else:
        lw1 = dma(ctx, "gpsimd", t["w_k"][:, :, 0:128],
                  d["l3_lkv"].rearrange("(a p) m -> p a m", p=128),
                  deps=[war_pe], dsem="sm_wk")
        lw2 = None
    lwq = dma(ctx, "gpsimd", t["w_q"][:],
              d["l3_gwq" if glob else "l3_lwq"].rearrange("(a p) m -> p a m", p=128),
              deps=[war_pe], dsem="sm_wq")
    ago = d["ag0_out"]
    xs = t["xn_strm"]
    pa, pb = t["ps_a"], t["ps_b"]
    trbf = t["ps_s10"][:].bitcast(BF16)
    kvreg = [t["ps_s00"][:, 0:128], t["ps_s00"][:, 128:256]]
    qreg = [t["ps_s01"][:, 0:256], t["ps_s01"][:, 256:512]]
    kbufF = t["x_stage"][:, 0, 0:64]
    kbuf = kbufF.rearrange("p (k two) -> p k two", two=2)
    qbufF = t["x_stage"][:, 1, 0:256]
    qbuf = qbufF.rearrange("p (k two) -> p k two", two=2)
    A32 = t["scr_f32"][:, 0:32].rearrange("p (f z) -> p f z", z=1)
    B32 = t["scr_f32"][:, 32:64].rearrange("p (f z) -> p f z", z=1)
    sqK = t["scr_f32"][:, 64:128]
    A128 = t["scr_f32"][:, 128:256].rearrange("p (f z) -> p f z", z=1)
    B128 = t["scr_f32"][:, 256:384].rearrange("p (f z) -> p f z", z=1)
    sqQ = t["scr_f32"][:, 384:640]
    qn = t["qn_st"]
    xs_free = None
    rp_free = [None, None]
    la_t = [None, None]
    kv_free = [None, None]
    q_free = [None, None]
    ktr_free = None
    qtr_free = [None, None]
    qd_prev = [None, None]
    qd = None
    first_dve = [war_dve]
    for w in range(NW):
        j = (w * 512) // T
        lc = w * 512 - j * T
        sl = w % 2
        ldx = dma(ctx, "sync", xs[:, 0, :, :],
                  ago[j * D:(j + 1) * D, lc:lc + 512].rearrange("(a p) t -> p a t", p=128),
                  deps=[ag_done, xs_free, war_pe if w == 0 else None], dsem="sm_xnstrm0")
        ldr = dma(ctx, "gpsimd", t["rope_sb"][:, sl * 4:(sl + 1) * 4, :],
                  tab[:, w * 4:(w + 1) * 4, :], deps=[rp_free[sl]], dsem=f"sm_rope{sl}")
        if glob:
            for mc, psL in ((0, pa), (1, pb)):
                mmn = None
                for kc in range(8):
                    mmn = o("tensor", lambda e, mc=mc, kc=kc, psL=psL: e.matmul(
                        psL[:], t["w_v"][:, kc, mc * 128:(mc + 1) * 128], xs[:, 0, kc, :],
                        start=(kc == 0), stop=(kc == 7)),
                        deps=[ldx, lw1, la_t[mc], t.get("psab_free") if w == 0 else None]
                        if kc == 0 else [])
                la_t[mc] = o(v, lambda e, mc=mc, psL=psL: e.tensor_copy(
                    t["l1f_g"][:, mc, :], psL[:]), deps=[mmn, kv_free[0], kv_free[1]])
        for tc in range(4):
            ch = w * 4 + tc
            par = ch % 2
            idx = sl * 4 + tc
            if glob:
                kvm = None
                for mc in (0, 1):
                    kvm = o("tensor", lambda e, mc=mc, tc=tc, par=par: e.matmul(
                        kvreg[par], t["l1f_g"][:, mc, tc * 128:(tc + 1) * 128],
                        t["kvup_sb"][:, mc, :], start=(mc == 0), stop=(mc == 1)),
                        deps=[la_t[0], la_t[1], lw2, kv_free[par]] if mc == 0 else [])
            else:
                kvm = None
                for kc in range(8):
                    kvm = o("tensor", lambda e, kc=kc, tc=tc, par=par: e.matmul(
                        kvreg[par], xs[:, 0, kc, tc * 128:(tc + 1) * 128],
                        t["w_k"][:, kc, 0:128], start=(kc == 0), stop=(kc == 7)),
                        deps=[ldx, lw1, kv_free[par]] if kc == 0 else [])
            qm = None
            for kc in range(8):
                qm = o("tensor", lambda e, kc=kc, tc=tc, par=par: e.matmul(
                    qreg[par], xs[:, 0, kc, tc * 128:(tc + 1) * 128],
                    t["w_q"][:, kc, 0:256], start=(kc == 0), stop=(kc == 7)),
                    deps=[ldx, lwq, q_free[par]] if kc == 0 else [])
            kv2 = kvreg[par][:, 0:64].rearrange("p (k two) -> p k two", two=2)
            cosK = t["rope_sb"][:, idx:idx + 1, 0:32].rearrange("p a f -> p f a")
            sinK = t["rope_sb"][:, idx:idx + 1, 128:160].rearrange("p a f -> p f a")
            dd = o(v, lambda e, kv2=kv2, cosK=cosK: e.tensor_tensor(
                out=A32, in0=kv2[:, :, 0:1], in1=cosK, op=ALU.mult),
                deps=[kvm, ldr] + first_dve)
            first_dve = []
            dd = o(v, lambda e, kv2=kv2, sinK=sinK: e.tensor_tensor(
                out=B32, in0=kv2[:, :, 1:2], in1=sinK, op=ALU.mult), deps=[dd])
            dd = o(v, lambda e: e.tensor_tensor(
                out=kbuf[:, :, 0:1], in0=A32, in1=B32, op=ALU.subtract), deps=[dd])
            dd = o(v, lambda e, kv2=kv2, sinK=sinK: e.tensor_tensor(
                out=A32, in0=kv2[:, :, 0:1], in1=sinK, op=ALU.mult), deps=[dd])
            r5 = o(v, lambda e, kv2=kv2, cosK=cosK: e.tensor_tensor(
                out=B32, in0=kv2[:, :, 1:2], in1=cosK, op=ALU.mult), deps=[dd])
            dd = o(v, lambda e: e.tensor_tensor(
                out=kbuf[:, :, 1:2], in0=A32, in1=B32, op=ALU.add), deps=[r5])
            dd = o(v, lambda e: e.tensor_tensor(out=sqK, in0=kbufF, in1=kbufF,
                                                op=ALU.mult), deps=[dd])
            dd = o(v, lambda e: e.tensor_reduce(qn[:, 0:1], sqK, AX.X, ALU.add), deps=[dd])
            dd = o(v, lambda e: e.tensor_scalar(out=qn[:, 1:2], in0=qn[:, 0:1],
                                                scalar1=1.0 / 64.0, scalar2=RMS_EPS,
                                                op0=ALU.mult, op1=ALU.add), deps=[dd])
            r10 = o(v, lambda e: e.reciprocal(qn[:, 2:3], qn[:, 1:2]), deps=[dd])
            r11 = o("scalar", lambda e: e.activation(qn[:, 3:4], qn[:, 2:3], AF.Sqrt),
                    deps=[r10])
            r12 = o(v, lambda e: e.tensor_scalar(
                out=t["kq_bf"][:, 0:64], in0=kbufF, scalar1=qn[:, 3:4], scalar2=None,
                op0=ALU.mult), deps=[r11])
            vc = r12
            for h in range(4):
                vc = o(v, lambda e, h=h, ch=ch, par=par: e.tensor_copy(
                    t["v_aug"][:, ch, h, 0:64], kvreg[par][:, 64:128]),
                    deps=[vc, war_pe if (h == 0 and ch == 0) else None])
            kv_free[par] = vc
            ktr = o("tensor", lambda e: e.transpose(
                trbf[0:64, 0:128], t["kq_bf"][:, 0:64], t["idn_bf"][:]),
                deps=[r12, ktr_free, war_dve if ch == 0 else None])
            kcp = vc
            for hp in (0, 1):
                for hf in (0, 1):
                    kcp = o(v, lambda e, hp=hp, hf=hf, ch=ch: e.tensor_copy(
                        t["kt_fm"][hf * 64:(hf + 1) * 64, hp, ch * 128:(ch + 1) * 128],
                        trbf[0:64, 0:128]),
                        deps=[kcp, ktr] + ([t.get("ktfree"), war_pe] if (
                            hp == 0 and hf == 0 and ch == 0) else []))
            ktr_free = kcp
            qv = qreg[par].rearrange("p (k two) -> p k two", two=2)
            cosQ = t["rope_sb"][:, idx:idx + 1, 0:128].rearrange("p a f -> p f a")
            sinQ = t["rope_sb"][:, idx:idx + 1, 128:256].rearrange("p a f -> p f a")
            dd = o(v, lambda e, qv=qv, cosQ=cosQ: e.tensor_tensor(
                out=A128, in0=qv[:, :, 0:1], in1=cosQ, op=ALU.mult), deps=[qm, kcp])
            dd = o(v, lambda e, qv=qv, sinQ=sinQ: e.tensor_tensor(
                out=B128, in0=qv[:, :, 1:2], in1=sinQ, op=ALU.mult), deps=[dd])
            dd = o(v, lambda e: e.tensor_tensor(
                out=qbuf[:, :, 0:1], in0=A128, in1=B128, op=ALU.subtract), deps=[dd])
            dd = o(v, lambda e, qv=qv, sinQ=sinQ: e.tensor_tensor(
                out=A128, in0=qv[:, :, 0:1], in1=sinQ, op=ALU.mult), deps=[dd])
            s5 = o(v, lambda e, qv=qv, cosQ=cosQ: e.tensor_tensor(
                out=B128, in0=qv[:, :, 1:2], in1=cosQ, op=ALU.mult), deps=[dd])
            dd = o(v, lambda e: e.tensor_tensor(
                out=qbuf[:, :, 1:2], in0=A128, in1=B128, op=ALU.add), deps=[s5])
            q_free[par] = s5
            dd = o(v, lambda e: e.tensor_tensor(out=sqQ, in0=qbufF, in1=qbufF,
                                                op=ALU.mult), deps=[dd])
            dd = o(v, lambda e: e.tensor_reduce(
                t["gn_sc"][:, 0:4].rearrange("p (h z) -> p h z", z=1),
                sqQ.rearrange("p (h dd) -> p h dd", h=4), AX.X, ALU.add), deps=[dd])
            dd = o(v, lambda e: e.tensor_scalar(out=t["gnwb_sb"][:, 0:4], in0=t["gn_sc"][:, 0:4],
                                                scalar1=1.0 / 64.0, scalar2=RMS_EPS,
                                                op0=ALU.mult, op1=ALU.add), deps=[dd])
            s10 = o(v, lambda e: e.reciprocal(t["gn_sc"][:, 0:4], t["gnwb_sb"][:, 0:4]),
                    deps=[dd])
            s11 = o("scalar", lambda e: e.activation(qn[:, 4:8], t["gn_sc"][:, 0:4], AF.Sqrt),
                    deps=[s10, r11])
            s12 = o(v, lambda e: e.tensor_tensor(
                out=t["kq_bf"][:, 128:384].rearrange("p (h dd) -> p h dd", h=4),
                in0=qbufF.rearrange("p (h dd) -> p h dd", h=4),
                in1=qn[:, 4:8].rearrange("p (h z) -> p h z", z=1).broadcast_to([128, 4, 64]),
                op=ALU.mult), deps=[s11, dd])
            for cx in (0, 1):
                qtr = o("tensor", lambda e, cx=cx: e.transpose(
                    trbf[:, 256 + cx * 128:256 + (cx + 1) * 128],
                    t["kq_bf"][:, 128 + cx * 128:128 + (cx + 1) * 128], t["idn_bf"][:]),
                    deps=[s12, qtr_free[cx]])
                qcp = o(v, lambda e, cx=cx: e.tensor_copy(
                    t["q_stg"][:, cx * 128:(cx + 1) * 128],
                    trbf[:, 256 + cx * 128:256 + (cx + 1) * 128]),
                    deps=[qtr, qd_prev[cx]])
                qtr_free[cx] = qcp
                qd = o("scalar", lambda e, cx=cx, ch=ch: e.dma_start(
                    out=ctx.dram["q_scr"][cx * 128:(cx + 1) * 128, ch * 128:(ch + 1) * 128],
                    in_=t["q_stg"][:, cx * 128:(cx + 1) * 128]),
                    deps=[qcp, qd, war_qscr if ch == 0 else None], dma=True,
                    dsem=f"sm_qscr")
                qd_prev[cx] = qd
        xs_free = qm
        rp_free[sl] = s5
    t["qscr_last"] = qd
    t["qkv_done"] = ctx.now("dve")[0]


def emit_layer3(ctx, x_src, x_dst):
    cfg, t, d = ctx.cfg, ctx.t, ctx.dram
    S, T, TT = cfg.S, cfg.T, cfg.TT
    ACCS = ("d_pool", "d_scal", "sm_seed", "sm_ffnacc0", "sm_ffnacc1", "sm_rsacc0",
            "sm_rsacc1", "sm_l1acc0", "sm_l1acc1", "sm_l2acc0", "sm_l2acc1",
            "sm_moeacc0", "sm_moeacc1", "sm_tacc0")
    prior = ctx.now(*ACCS) + [t.get("layer_done")]
    seed_t = [dma(ctx, "sync", x_dst[i * 128:(i + 1) * 128, :],
                  x_src[i * 128:(i + 1) * 128, :], deps=prior, dsem="sm_seed")
              for i in range(TT)]
    seed_all = seed_t[-1]
    t["seed_all"] = seed_all
    t["ffn_acc_all0"] = t["ffn_acc_all1"] = None
    nt = emit_norm(ctx, x_src, kind="ln", deps=prior, store32="xn32b_scr")
    tseed = dma(ctx, "gpsimd", d["t_scr"][:, :], d["xn32b_scr"][:, :],
                deps=[nt["s32"][-1]], dsem="sm_xn32")
    moe = emit_l3_moe(ctx, x_dst, nt, seed_all)
    rn1 = emit_norm(ctx, d["xn32b_scr"], kind="rms", w_row_ap=d["l3_n1w"],
                    deps=[moe["pe"], moe["dve"], nt["s32"][-1]],
                    war_deps=[moe["pe"], moe["dve"]])
    ag1 = emit_allgather_xn(ctx, "ag0", deps=rn1["fm"])
    t["ktfree"] = moe["pe"]
    peg = ctx.now("pe")[0]
    emit_l3_kvq(ctx, "g", ag_done=ag1, war_pe=peg, war_dve=moe["dve"])
    ydg = emit_attention(ctx, "l3g", wo_name="l3_gwo", bo_name="l3_zb")
    rsg = emit_rs_residual(ctx, d["t_scr"], ydg, seed_t, acc_deps=[tseed])
    pe2 = ctx.now("pe")[0]
    dve2 = ctx.now("dve")[0]
    rn2 = emit_norm(ctx, d["t_scr"], kind="rms", w_row_ap=d["l3_n2w"],
                    deps=rsg + [pe2, dve2] + ctx.now("sm_agst"),
                    war_deps=[pe2, dve2])
    ag2 = emit_allgather_xn(ctx, "ag0", deps=rn2["fm"])
    qsw = ctx.now("sm_qpad0", "sm_qpad1")
    t["ktfree"] = None
    pel = ctx.now("pe")[0]
    emit_l3_kvq(ctx, "l", ag_done=ag2, war_pe=pel, war_dve=dve2,
                war_qscr=qsw[0] if qsw else None)
    ydl = emit_attention(ctx, "l3l", wo_name="l3_lwo", bo_name="l3_zb")
    rsl = emit_rs_residual(ctx, x_dst, ydl, seed_t,
                           acc_deps=[seed_all] + moe["accs"])
    rstot = ctx.now("sm_rsacc0", "sm_rsacc1")
    prev = None
    for i in range(TT):
        ld = dma(ctx, "sync", t["rs_f32"][:, 0, :], d["t_scr"][i * 128:(i + 1) * 128, :],
                 deps=[prev, t.get("rs_buf_free") if i == 0 else None] +
                      (rsg if i == 0 else []), dsem="sm_rsbf0")
        ac = dma_acc(ctx, x_dst[i * 128:(i + 1) * 128, :], t["rs_f32"][:, 0, :],
                     deps=[ld] + (rstot if i == 0 else []), dsem="sm_l2acc0")
        prev = ac
    t["rs_buf_free"] = prev
    t["layer_done"] = ctx.now("pe")[0]





def emit_l0_gating(ctx, ar_done):
    t, d = ctx.t, ctx.dram
    pa, pb = t["ps_a"], t["ps_b"]
    lw = {}
    lw["sw2"] = dma(ctx, "gpsimd", t["w_g2"][:, 0, :, 0:3], d["l0_sw2T"].rearrange("(a p) m -> p a m", p=128), dsem="sm_wg2")
    for p in range(3):
        lw[f"pw2{p}"] = dma(ctx, "gpsimd", t["w_g2"][:, 1 + p, :, :],
                            d["l0_pw2T"][p].rearrange("(a p2) m -> p2 a m", p2=128),
                            dsem="sm_wg2")
    lw_all = lw["pw22"]

    xm_ld = t["xm_ld"]
    xmbf = ctx.op("vector", lambda e: e.tensor_copy(t["xm_bf"][:], t["xm_fm"][:]),
                  deps=[xm_ld])
    zfm_t = {}
    grow_free = {}
    row_t = {}
    for zi in range(4):
        sl = zi % 2
        b1src = d["l0_sb1"] if zi == 0 else d["l0_pb1"][zi - 1]
        b2src = d["l0_sb2"] if zi == 0 else d["l0_pb2"][zi - 1]
        ncols = 3 if zi == 0 else 16
        lb1 = dma(ctx, "gpsimd", t["g_rows"][0:1, sl, 0:512], b1src,
                  deps=[grow_free.get(zi - 2)], dsem=f"sm_gr{sl}a")
        lb2 = dma(ctx, "gpsimd", t["g_rows"][0:1, sl, 512:512 + ncols], b2src,
                  deps=[grow_free.get(zi - 2)], dsem=f"sm_gr{sl}b")
        if zi == 0:
            srcw = d["l0_sw1T"].rearrange("(a p) m -> p a m", p=128)
        else:
            srcw = d["l0_pw1T"][zi - 1].rearrange("(a p2) m -> p2 a m", p2=128)
        wdep = dma(ctx, "gpsimd", t["w_rhs"][:, :, 0:512], srcw,
                   deps=[zfm_t.get(("wfree", zi - 1))], dsem="sm_wgat")
        mm = None
        for kc in range(8):
            mm = ctx.op("tensor", lambda e, zi=zi, kc=kc: e.matmul(
                pa[0:1, :], t["xm_bf"][:, kc:kc + 1], t["w_rhs"][:, kc, 0:512],
                start=(kc == 0), stop=(kc == 7)),
                deps=[xmbf, wdep, t.get("psab_free"), zfm_t.get(("pa_free", zi - 1))] if kc == 0 else [])
        zfm_t[("wfree", zi)] = mm
        badd = ctx.op("vector", lambda e, zi=zi: e.tensor_tensor(
            out=t["g_row"][0:1, :], in0=pa[0:1, :],
            in1=t["g_rows"][0:1, zi % 2, 0:512], op=ALU.add),
            deps=[mm, lb1])
        zfm_t[("pa_free", zi)] = badd
        gl = ctx.op("scalar", lambda e: e.activation(t["g_row"][0:1, :], t["g_row"][0:1, :],
                                                     AF.Gelu), deps=[badd])
        lastt = None
        for c in range(4):
            lastt = ctx.op("tensor", lambda e, zi=zi, c=c: e.transpose(
                pb[:, c:c + 1], t["g_row"][0:1, c * 128:(c + 1) * 128], t["idn_f32"][0:1, 0:1]),
                deps=[gl, t["idnf_ld"], t.get("psab_free2"), zfm_t.get(("zf", zi - 1))] if c == 0 else [gl])
        zf = ctx.op("vector", lambda e, zi=zi: e.tensor_copy(t["z_fm"][:, zi, :], pb[:, 0:4]),
                    deps=[lastt])
        zfm_t[("zf", zi)] = zf
        t["psab_free2"] = zf
        mm = None
        for kc in range(4):
            mm = ctx.op("tensor", lambda e, zi=zi, kc=kc, ncols=ncols: e.matmul(
                pa[0:1, 0:ncols], t["z_fm"][:, zi, kc:kc + 1], t["w_g2"][:, zi, kc, 0:ncols],
                start=(kc == 0), stop=(kc == 3)),
                deps=[zf, lw_all, zfm_t[("pa_free", zi)]] if kc == 0 else [])
        badd2 = ctx.op("vector", lambda e, zi=zi, ncols=ncols: e.tensor_tensor(
            out=t["g2_rows"][0:1, zi, 0:ncols], in0=pa[0:1, 0:ncols],
            in1=t["g_rows"][0:1, zi % 2, 512:512 + ncols], op=ALU.add),
            deps=[mm, lb2])
        grow_free[zi] = badd2
        zfm_t[("pa_free", zi)] = badd2
        mx = ctx.op("vector", lambda e, zi=zi, ncols=ncols: e.tensor_reduce(
            t["g_sm"][0:1, 0:1], t["g2_rows"][0:1, zi, 0:ncols], AX.X, ALU.max), deps=[badd2])
        sb2_ = ctx.op("vector", lambda e, zi=zi, ncols=ncols: e.tensor_scalar(
            out=t["g2_rows"][0:1, zi, 0:ncols], in0=t["g2_rows"][0:1, zi, 0:ncols],
            scalar1=t["g_sm"][0:1, 0:1], scalar2=None, op0=ALU.subtract), deps=[mx])
        ex = ctx.op("scalar", lambda e, zi=zi, ncols=ncols: e.activation(
            t["g2_rows"][0:1, zi, 0:ncols], t["g2_rows"][0:1, zi, 0:ncols], AF.Exp), deps=[sb2_])
        sm = ctx.op("vector", lambda e, zi=zi, ncols=ncols: e.tensor_reduce(
            t["g_sm"][0:1, 1:2], t["g2_rows"][0:1, zi, 0:ncols], AX.X, ALU.add), deps=[ex])
        rc = ctx.op("vector", lambda e: e.reciprocal(t["g_sm"][0:1, 2:3], t["g_sm"][0:1, 1:2]),
                    deps=[sm])
        row_t[zi] = ctx.op("vector", lambda e, zi=zi, ncols=ncols: e.tensor_scalar(
            out=t["g2_rows"][0:1, zi, 0:ncols], in0=t["g2_rows"][0:1, zi, 0:ncols],
            scalar1=t["g_sm"][0:1, 2:3], scalar2=None, op0=ALU.mult), deps=[rc])
        row_t[("free", zi)] = row_t[zi]

    hg0 = ctx.op("vector", lambda e: e.tensor_scalar(
        out=t["g_hg"][0:1, :], in0=t["g2_rows"][0:1, 1, 0:16],
        scalar1=t["g2_rows"][0:1, 0, 0:1], scalar2=None, op0=ALU.mult),
        deps=[row_t[0], row_t[1]])
    for p in (1, 2):
        m = ctx.op("vector", lambda e, p=p: e.tensor_scalar(
            out=t["g_tmp"][0:1, :], in0=t["g2_rows"][0:1, 1 + p, 0:16],
            scalar1=t["g2_rows"][0:1, 0, p:p + 1], scalar2=None, op0=ALU.mult),
            deps=[row_t[0], row_t[1 + p], hg0])
        hg0 = ctx.op("vector", lambda e: e.tensor_tensor(
            out=t["g_hg"][0:1, :], in0=t["g_hg"][0:1, :], in1=t["g_tmp"][0:1, :],
            op=ALU.add), deps=[m])
    tr = None
    for c in range(1):
        tr = ctx.op("tensor", lambda e: e.transpose(pb[0:16, 4:5],
                                                    t["g_hg"][0:1, :], t["idn_f32"][0:1, 0:1]),
                    deps=[hg0, t.get("psab_free2")])
    cph = ctx.op("vector", lambda e: e.tensor_copy(t["g_hgc"][0:16, 0:1], pb[0:16, 4:5]), deps=[tr])
    t["psab_free2"] = cph
    mm = ctx.op("tensor", lambda e: e.matmul(pa[0:1, 0:4], t["g_hgc"][0:16, 0:1], t["hsel"][:],
                                             start=True, stop=True),
                deps=[cph, t["hsel_ld"], row_t.get(("free", 3))])
    sc = ctx.op("vector", lambda e: e.tensor_scalar(
        out=t["g_hgo"][0:1, 0:4], in0=pa[0:1, 0:4], scalar1=0.125, scalar2=None, op0=ALU.mult),
        deps=[mm])
    bcm = ctx.op("tensor", lambda e: e.matmul(pa[:, 0:4], t["ones1"][:], t["g_hgo"][0:1, 0:4],
                                              start=True, stop=True), deps=[sc])
    hgd = ctx.op("vector", lambda e: e.tensor_copy(t["hg_sc"][:], pa[:, 0:4]), deps=[bcm])
    t["hg_done"] = hgd
    t["psab_free"] = hgd
    return hgd


def emit_l0_xm(ctx, fm_t):
    cfg, t, d = ctx.cfg, ctx.t, ctx.dram
    red = []
    for c in range(8):
        red.append(ctx.op("vector", lambda e, c=c: e.tensor_reduce(
            t["xm_fm"][:, c:c + 1], t["xn_fm"][:, c, :], AX.X, ALU.add), deps=fm_t))
    sc = ctx.op("vector", lambda e: e.tensor_scalar(
        out=t["xm_fm"][:], in0=t["xm_fm"][:], scalar1=1.0 / cfg.S, scalar2=None,
        op0=ALU.mult), deps=red)
    st = dma(ctx, "gpsimd", d["ar_in"].rearrange("o (c p) -> p (o c)", p=128),
             t["xm_fm"][:], deps=[sc], slow=True, dsem="sm_arst")
    cc = ctx.op("gpsimd", lambda e: e.collective_compute(
        "AllReduce", ALU.add, replica_groups=GROUPS4,
        ins=[d["ar_in"].opt()], outs=[d["ar_out"].opt()]), deps=[st], cc=True,
        dsem="cc_ar")
    ld = dma(ctx, "gpsimd", t["xm_fm"][:], d["ar_out"].rearrange("o (c p) -> p (o c)", p=128),
             deps=[cc], slow=True, dsem="sm_arld")
    t["xm_ld"] = ld
    return ld


def emit_l0_ffn(ctx, x_dst, fm_t, seed_t):
    cfg, t, d = ctx.cfg, ctx.t, ctx.dram
    TT, T = ctx.cfg.TT, ctx.cfg.T
    pa, pb = t["ps_a"], t["ps_b"]

    lk = dma(ctx, "sync", t["x_stage"][:, 0, :], d["l0_kern_sl"],
             deps=[t.get("xstage_free")], dsem="sm_kq")
    am1 = ctx.op("vector", lambda e: e.tensor_reduce(
        t["stats"][:, 7:8], t["x_stage"][:, 0, :], AX.X, ALU.max, apply_absolute_value=True),
        deps=[lk])
    trm = ctx.op("tensor", lambda e: e.transpose(pb[0:1, 0:128], t["stats"][:, 7:8],
                                                 t["idn_f32"][:]),
                 deps=[am1, t.get("psab_free2"), t["idnf_ld"]])
    cpm = ctx.op("vector", lambda e: e.tensor_copy(t["g_row"][0:1, 0:128], pb[0:1, 0:128]),
                 deps=[trm])
    t["psab_free2"] = cpm
    am2 = ctx.op("vector", lambda e: e.tensor_reduce(
        t["g_sm"][0:1, 3:4], t["g_row"][0:1, 0:128], AX.X, ALU.max), deps=[cpm])
    amb = ctx.op("vector", lambda e: e.tensor_copy(
        t["g_row"][0:1, 128:136], t["g_sm"][0:1, 3:4].broadcast_to([1, 8])), deps=[am2])
    stm = dma(ctx, "gpsimd", d["arm_in"][0:1, 0:8], t["g_row"][0:1, 128:136], deps=[amb], dsem="sm_armst")
    ccm = ctx.op("gpsimd", lambda e: e.collective_compute(
        "AllReduce", ALU.max, replica_groups=GROUPS8,
        ins=[d["arm_in"].opt()], outs=[d["arm_out"].opt()]), deps=[stm], cc=True,
        dsem="cc_arm")
    ldm = dma(ctx, "gpsimd", t["g_sm"][0:1, 4:5], d["arm_out"][0:1, 0:1], deps=[ccm], dsem="sm_armld")
    s1 = ctx.op("vector", lambda e: e.tensor_scalar(
        out=t["g_sm"][0:1, 5:6], in0=t["g_sm"][0:1, 4:5], scalar1=1.0 / 7.0, scalar2=None,
        op0=ALU.mult), deps=[ldm])
    s2 = ctx.op("vector", lambda e: e.reciprocal(t["g_sm"][0:1, 6:7], t["g_sm"][0:1, 5:6]),
                deps=[s1])
    bc1 = ctx.op("tensor", lambda e: e.matmul(pa[:, 6:7], t["ones1"][:], t["g_sm"][0:1, 5:6],
                                              start=True, stop=True),
                 deps=[s2, t.get("psab_free")])
    bc2 = ctx.op("tensor", lambda e: e.matmul(pa[:, 7:8], t["ones1"][:], t["g_sm"][0:1, 6:7],
                                              start=True, stop=True), deps=[s2])
    bcc = ctx.op("vector", lambda e: e.tensor_copy(t["qsc"][:], pa[:, 6:8]), deps=[bc1, bc2])
    t["psab_free"] = bcc
    q1 = ctx.op("vector", lambda e: e.tensor_scalar(
        out=t["x_stage"][:, 0, :], in0=t["x_stage"][:, 0, :], scalar1=t["qsc"][:, 1:2],
        scalar2=RND_MAGIC, op0=ALU.mult, op1=ALU.add), deps=[bcc, am1])
    q2 = ctx.op("vector", lambda e: e.tensor_scalar(
        out=t["x_stage"][:, 0, :], in0=t["x_stage"][:, 0, :], scalar1=RND_MAGIC, scalar2=7.0,
        op0=ALU.subtract, op1=ALU.min), deps=[q1])
    q3 = ctx.op("vector", lambda e: e.tensor_scalar(
        out=t["x_stage"][:, 0, :], in0=t["x_stage"][:, 0, :], scalar1=-8.0, scalar2=t["qsc"][:, 0:1],
        op0=ALU.max, op1=ALU.mult), deps=[q2])
    q4 = ctx.op("vector", lambda e: e.tensor_copy(t["kq_bf"][:], t["x_stage"][:, 0, :]), deps=[q3])
    stq = dma(ctx, "gpsimd", d["agw_in"][:, :], t["kq_bf"][:], deps=[q4], dsem="sm_agwst")
    t["xstage_free"] = q4
    ccq = ctx.op("gpsimd", lambda e: e.collective_compute(
        "AllGather", ALU.bypass, replica_groups=GROUPS8,
        ins=[d["agw_in"].opt()], outs=[d["agw_out"].opt()]), deps=[stq], cc=True,
        dsem="cc_agw")

    la = dma(ctx, "gpsimd", t["lora_a"][:], d["l0_a"].rearrange("(a p) m -> p a m", p=128), dsem="sm_lra")
    lbm = dma(ctx, "gpsimd", t["lora_b"][:], d["l0_b"], dsem="sm_lrb")
    lfb = dma(ctx, "gpsimd", t["brow_bf2"][:], d["l0_fbias"], dsem="sm_fbias")
    xa_t = {}
    for tq in range(T // 512):
        mm = None
        for kc in range(8):
            mm = ctx.op("tensor", lambda e, tq=tq, kc=kc: e.matmul(
                pa[0:16, :], t["lora_a"][:, kc, :], t["xn_fm"][:, kc, tq * 512:(tq + 1) * 512],
                start=(kc == 0), stop=(kc == 7)),
                deps=[la, fm_t[4 * tq + 3], t.get("psab_free"), xa_t.get(tq - 1)] if kc == 0 else [])
        xa_t[tq] = ctx.op("vector", lambda e, tq=tq: e.tensor_copy(
            t["lora_x"][:, tq * 512:(tq + 1) * 512], pa[0:16, :]), deps=[mm])
        t["psab_free"] = xa_t[tq]

    ldw = {}
    for tt in range(TT):
        buf = tt % 2
        pass
    lwq = dma(ctx, "gpsimd", t["w_rhs"][:], d["agw_out"].rearrange("(a p) m -> p a m", p=128),
              deps=[ccq, t.get("w_rhs_free"), t.get("hg_done")], dsem="sm_wrhs")
    acc_t = {}
    for tt in range(TT):
        tq, tc = tt // 4, tt % 4
        for nb in (0, 1):
            ps = pa if nb == 0 else pb
            dep0 = [lwq, lfb, lbm, xa_t[tq], t.get("psab_free"), t.get("psab_free2"),
                    fm_t[tt]]
            mm = None
            for kc in range(8):
                mm = ctx.op("tensor", lambda e, tt=tt, kc=kc, nb=nb, ps=ps: e.matmul(
                    ps[:], t["xn_fm"][:, kc, tt * 128:(tt + 1) * 128],
                    t["w_rhs"][:, kc, nb * 512:(nb + 1) * 512],
                    start=(kc == 0), stop=False), deps=dep0 if kc == 0 else [])
            mm = ctx.op("tensor", lambda e, nb=nb, ps=ps: e.matmul(
                ps[:], t["ones1b"][:], t["brow_bf2"][:, nb * 512:(nb + 1) * 512],
                start=False, stop=False), deps=[])
            mm = ctx.op("tensor", lambda e, tt=tt, nb=nb, ps=ps: e.matmul(
                ps[:], t["lora_x"][:, tt * 128:(tt + 1) * 128],
                t["lora_b"][:, nb * 512:(nb + 1) * 512], start=False, stop=True), deps=[])
            ev = ctx.op("vector", lambda e, nb=nb, ps=ps: e.tensor_copy(
                t["evac"][:, nb, :], ps[:]), deps=[mm, acc_t.get((tt - 1, nb))])
            t["psab_free" if nb == 0 else "psab_free2"] = ev
            acc = dma_acc(ctx, x_dst[tt * 128:(tt + 1) * 128, nb * 512:(nb + 1) * 512],
                          t["evac"][:, nb, :], deps=[ev, t["seed_all"]],
                          dsem=f"sm_ffnacc{nb}")
            t[("ffn_acc", tt, nb)] = acc
            acc_t[(tt, nb)] = acc
    t["w_rhs_free"] = acc_t[(TT - 1, 1)]
    t["ffn_acc_all0"] = acc_t[(TT - 1, 0)]
    t["ffn_acc_all1"] = acc_t[(TT - 1, 1)]
    return acc_t



def emit_layer0(ctx, x_src, x_dst):
    cfg, t = ctx.cfg, ctx.t
    TT = cfg.TT
    prior = ctx.now("d_pool", "d_scal", "sm_seed", "sm_ffnacc0", "sm_ffnacc1",
                    "sm_rsacc0", "sm_rsacc1", "sm_l1acc0", "sm_l1acc1",
                    "sm_l2acc0", "sm_l2acc1") + [t.get("layer_done")]
    seed_t = [dma(ctx, "sync", x_dst[i * 128:(i + 1) * 128, :],
                  x_src[i * 128:(i + 1) * 128, :], deps=prior, dsem="sm_seed")
              for i in range(TT)]
    t["seed_all"] = seed_t[-1]
    nt = emit_norm(ctx, x_src, kind="ln", deps=prior)
    ag_done = emit_allgather_xn(ctx, "ag0", deps=nt["fm"])
    emit_l0_xm(ctx, nt["fm"])
    emit_l0_gating(ctx, None)
    t["qkv_done"] = emit_qkv(ctx, "l0_wqT", "l0_wkT", "l0_wvT",
                             "l0_bq", "l0_bk", "l0_bv", "ag0", ag_done)
    emit_l0_ffn(ctx, x_dst, nt["fm"], seed_t)
    ydeps = emit_attention(ctx, "l0", hg=True, wo_name="l0_woT", bo_name="l0_bo")
    emit_rs_residual(ctx, x_dst, ydeps, seed_t)
    t["l0_done"] = ctx.now("pe")[0]



def build(cfg: Cfg):
    nc = bass.Bass("TRN2", target_bir_lowering=False)
    specs = input_specs_for(cfg)
    dram_in = {k: nc.dram_tensor(k, sh, dt, kind="ExternalInput").ap()
               for k, (sh, dt) in specs.items()}
    S, T = cfg.S, cfg.T
    x_out = nc.dram_tensor("x_out", [T, D], F32, kind="ExternalOutput").ap()

    es = ExitStack()
    with es:
        ctx = Ctx(nc, es, cfg)
        ctx.dram.update(dram_in)
        ctx.dram["x_out"] = x_out
        dr = lambda n, sh, dt: ctx.dram.__setitem__(
            n, nc.dram_tensor(n, sh, dt, kind="Internal").ap())
        dr("x_scr_a", [T, D], F32)
        dr("x_scr_b", [T, D], F32)
        dr("ag0_in", [D, T], BF16)
        dr("ag0_out", [4 * D, T], BF16)
        dr("y_part", [S, D], BF16)
        dr("q_scr", [256, S], BF16)
        if 1 in cfg.layers:
            dr("y_scr", [256, S], BF16)
            dr("xn32_scr", [T, D], F32)
            dr("xq_scr", [D, T], BF16)
            dr("h_scr", [T, 2816], F32)
            dr("hq_scr", [22, cfg.TT, 128, 128], BF16)
            dr("wgq", [8, 22, 128, 128], BF16)
            dr("wuq", [8, 22, 128, 128], BF16)
            dr("wdq", [22, 128, D], BF16)
        if 2 in cfg.layers:
            dr("bfa", [4096, 128], F32)
            dr("bfb", [4096, 128], F32)
            dr("agu_in", [128, 4096], BF16)
            dr("agu_out", [1024, 4096], BF16)
            dr("agd_in", [4096, 128], BF16)
            dr("agd_out", [8 * 4096, 128], BF16)
            dr("h2_scr", [32, 128, T], BF16)
        if 3 in cfg.layers:
            dr("xn32b_scr", [T, D], F32)
            dr("t_scr", [T, D], F32)
        dr("rs_out", [T, D], BF16)
        dr("ar_in", [1, D], F32)
        dr("ar_out", [1, D], F32)
        dr("arm_in", [1, 8], F32)
        dr("arm_out", [1, 8], F32)
        dr("agw_in", [128, D], BF16)
        dr("agw_out", [D, D], BF16)

        t = ctx.t
        sb = lambda n, sh, dt: t.__setitem__(n, es.enter_context(nc.sbuf_tensor(n, sh, dt)))
        ps = lambda n, sh, dt=F32: t.__setitem__(n, es.enter_context(nc.psum_tensor(n, sh, dt)))

        sb("x_stage", [128, 2, D], F32)
        sb("scr_f32", [128, D], F32)
        sb("stats", [128, 8], F32)
        sb("xn_tm", [128, 2, D], BF16)
        sb("xn_fm", [128, 8, T], BF16)
        sb("idn_bf", [128, 128], BF16)
        sb("idn_f32", [128, 128], F32)
        sb("ones1", [1, 128], F32)
        sb("ones1b", [1, 128], BF16)
        sb("ones_row", [1, 512], BF16)
        sb("bias_c", [128, 1], F32)
        sb("hsel", [NH, HPC], F32)
        sb("q_stg", [128, 512], BF16)
        sb("kt_fm", [128, 2, S], BF16)
        sb("v_aug", [128, S // 128, 4, 65], BF16)
        sb("q_pad", [128, 2, 512], BF16)
        sb("p_sb", [128, 2, 2, 512], BF16)
        sb("o2", [128, 2, 512], BF16)
        sb("rinv", [1, 2, 512], F32)
        sb("binv", [64, 2, 512], BF16)
        sb("w_q", [128, 8, 256], BF16)
        sb("w_k", [128, 8, 256], BF16)
        sb("w_v", [128, 8, 256], BF16)
        sb("b_q", [1, 256], BF16)
        sb("b_k", [1, 256], BF16)
        sb("b_v", [1, 256], BF16)
        sb("w_rhs", [128, 8, D], BF16)
        sb("wo_sl", [128, 2, D], BF16)
        sb("borow_bf", [1, D], BF16)
        sb("r0mask_sb", [1, 1], F32)
        sb("brow_bf2", [1, D], BF16)
        sb("xn_strm", [128, 1, 8, 512], BF16)
        sb("y_bf", [128, 2, 512], BF16)
        sb("rs_bf", [128, 1, D], BF16)
        sb("rs_f32", [128, 1, D], F32)
        if 3 in cfg.layers:
            t["wbc_f32"] = t["rs_f32"][:].rearrange("p a d -> p (a d)")
        sb("evac", [128, 2, 512], F32)
        if set(cfg.layers) & {1, 2, 3}:
            sb("gn_acc", [128, 4], F32)
            sb("gn_tmp", [128, 2], F32)
            sb("gn_sc", [128, 4], F32)
            sb("hsl_sb", [128, 2], F32)
            sb("hslT_sb", [2, 128], F32)
            sb("gnwb_sb", [128, 4], F32)
            sb("xq_mx", [128, 2, 176], F32)
        sb("arena", [128, 2816], F32)
        ar = t["arena"]
        t["cpt_sb"] = ar[:, 0:512].rearrange("p (a b) -> p a b", a=2)
        t["qd_bf"] = ar[:, 0:512].bitcast(BF16).rearrange("p (a b) -> p a b", a=2)
        t["q4b"] = ar[:, 512:1024]
        t["q4b2"] = ar[:, 1024:1536]
        t["w4src"] = ar[:, 1536:2048]
        t["q4stg"] = ar[:, 2048:2560].bitcast(BF16).rearrange("p (a b) -> p a b", a=8)
        t["l1f_g"] = ar[:, 2048:2560].bitcast(BF16).rearrange("p (a b) -> p a b", a=2)
        t["h_sl"] = ar[:, 2560:2816].bitcast(BF16)
        if 2 in cfg.layers:
            sb("gelu_b", [128, 64], F32)
        if 3 in cfg.layers:
            t["rope_sb"] = ar[:, 0:1024].bitcast(BF16).rearrange("p (a b) -> p a b", a=8)
            t["wrow"] = ar[0:1, 1024:2048]
            t["kvup_sb"] = ar[:, 1024:1152].bitcast(BF16).rearrange("p (a b) -> p a b", a=2)
            t["gw_sb"] = ar[:, 1152:1216].rearrange("p (a b) -> p a b", a=8)
            t["gwhi"] = ar[:, 1216:1248].bitcast(BF16).rearrange("p (a b) -> p a b", a=8)
            t["gwlo"] = ar[:, 1248:1280].bitcast(BF16).rearrange("p (a b) -> p a b", a=8)
            t["lg_sb"] = ar[:, 1280:1292]
            t["oh_full"] = ar[:, 1292:1292 + (T // 128) * 8]
            t["qn_st"] = ar[:, 1424:1432]
        sb("zero_c", [128, 1], F32)
        sb("hg_sc", [128, HPC], F32)
        sb("xm_fm", [128, 8], F32)
        sb("lora_x", [16, T], BF16)
        t["lora_b"] = ar[0:16, 0:512].bitcast(BF16)
        t["g_rows"] = ar[0:1, 1024:1552].bitcast(BF16).rearrange("p (a b) -> p a b", a=2)
        t["g_row"] = ar[0:1, 1552:2064]
        t["g2_rows"] = ar[0:1, 2064:2128].rearrange("p (a b) -> p a b", a=4)
        t["g_sm"] = ar[0:1, 2128:2136]
        t["g_hg"] = ar[0:1, 2136:2152]
        t["g_tmp"] = ar[0:1, 2152:2168]
        t["g_hgo"] = ar[0:1, 2168:2176]
        t["g_hgc"] = ar[0:16, 2176:2177]
        t["z_fm"] = ar[:, 2180:2188].bitcast(BF16).rearrange("p (a b) -> p a b", a=4)
        t["xm_bf"] = ar[:, 2188:2192].bitcast(BF16)
        t["w_g2"] = ar[:, 2192:2320].bitcast(BF16).rearrange("p (a b c) -> p a b c", a=4, b=4)
        t["lora_a"] = ar[:, 2700:2764].bitcast(BF16).rearrange("p (a b) -> p a b", a=8)
        sb("kq_bf", [128, D], BF16)
        sb("qsc", [128, 2], F32)

        ps("ps_a", [128, 512])
        ps("ps_b", [128, 512])
        ps("ps_s00", [128, 512])
        ps("ps_s01", [128, 512])
        ps("ps_s10", [128, 512])
        ps("ps_s11", [128, 512])
        ps("ps_o0", [128, 512])
        ps("ps_o1", [128, 512])

        nc.const_aps.register(F32, 0.0, t["zero_c"][:])
        emit_init(ctx)
        x_src = ctx.dram["x_in"]
        n_layers = len(cfg.layers)
        for li, L in enumerate(cfg.layers):
            last = (li == n_layers - 1)
            x_dst = ctx.dram["x_out"] if last else (
                ctx.dram["x_scr_a"] if li % 2 == 0 else ctx.dram["x_scr_b"])
            if L == 0:
                emit_layer0(ctx, x_src, x_dst)
            elif L == 1:
                emit_layer1(ctx, x_src, x_dst)
            elif L == 2:
                emit_layer2(ctx, x_src, x_dst)
            elif L == 3:
                emit_layer3(ctx, x_src, x_dst)
            else:
                raise NotImplementedError(f"layer {L}")
            x_src = x_dst

        allsems = [(n, v) for n, v in ctx.cnt.items()
                   if n.startswith(("d_", "sm_", "cc"))]
        ctx.wait_only("gpsimd", allsems)

        with nc.Block() as block:
            ctx.realize(block)
    return nc


def kernel(**inputs):
    cfg = Cfg(S=8192, layers=(0, 1, 2, 3))
    nc = build(cfg)
    in_maps = [prep_core_inputs(inputs, cfg, c) for c in range(8)]
    res = run_bass_kernel_spmd(nc, in_maps, core_ids=list(range(8)))
    out = np.empty((2, cfg.S, D), dtype=np.float32)
    for c in range(8):
        b, r = c // 4, c % 4
        out[b, r * cfg.T:(r + 1) * cfg.T, :] = np.asarray(res.results[c]["x_out"])
    return out
